# version 01
# baseline (speedup 1.0000x reference)
import numpy as np
import ml_dtypes
import concourse.bass as bass
import concourse.mybir as mybir
from concourse.alu_op_type import AluOpType
from concourse.bass_utils import run_bass_kernel_spmd
from contextlib import ExitStack

F32 = mybir.dt.float32
BF16 = mybir.dt.bfloat16
FP16 = mybir.dt.float16
AF = mybir.ActivationFunctionType
AX = mybir.AxisListType
OP = AluOpType

D = 4096
KC = 32
S2 = 2048
NT = 16
NS = 8
HN = 32
IB = 64
LC = 16
NE = 8
EPS = 1e-5
NEGBIG = -1.0e9
SCALE = 128.0 ** -0.5

UPTO = "P7"
TAPS = ()
SKIP_LN2T = False
P45_PARTS = "full"
PHASES = ["P1", "P2", "P3", "P4", "P45", "P56", "P7"]


def _slope(h):
    return 2.0 ** (-0.25 * (h + 1))


class Sem:
    def __init__(self, h):
        self.h = h
        self.n = 0

    def inc(self, instr, k=1):
        instr.then_inc(self.h, k)
        self.n += k
        return instr


def build_nc():
    upto = PHASES.index(UPTO)
    taps = set(TAPS)
    nc = bass.Bass("TRN2", target_bir_lowering=False)

    XB = nc.dram_tensor("XB", [NT, 128, D], F32, kind="ExternalInput")
    WQ4 = nc.dram_tensor("WQ4", [HN, KC, 128, 128], F32, kind="ExternalInput")
    WKV4 = nc.dram_tensor("WKV4", [2, KC, 128, 128], F32, kind="ExternalInput")
    WO4 = nc.dram_tensor("WO4", [8, KC, 128, 512], F32, kind="ExternalInput")
    UP4 = nc.dram_tensor("UP4", [IB, KC, 128, 128], F32, kind="ExternalInput")
    DN4 = nc.dram_tensor("DN4", [8, IB, 128, 512], F32, kind="ExternalInput")
    F14 = nc.dram_tensor("F14", [LC, KC, 128, 128], F32, kind="ExternalInput")
    EX4 = nc.dram_tensor("EX4", [4, LC, 128, 512], F32, kind="ExternalInput")
    F24 = nc.dram_tensor("F24", [8, LC, 128, 512], F32, kind="ExternalInput")
    GTH = nc.dram_tensor("GTH", [128, KC * NE], F32, kind="ExternalInput")
    GAMMA = nc.dram_tensor("GAMMA", [1, D], F32, kind="ExternalInput")
    BETA = nc.dram_tensor("BETA", [1, D], F32, kind="ExternalInput")
    IDBH = nc.dram_tensor("IDBH", [128, 128], BF16, kind="ExternalInput")
    MTRI = nc.dram_tensor("MTRI", [128, 128], F32, kind="ExternalInput")
    MROL = nc.dram_tensor("MROL", [128, 128], F32, kind="ExternalInput")
    KL2 = nc.dram_tensor("KL2", [2, S2], F32, kind="ExternalInput")
    Q2 = nc.dram_tensor("Q2", [2, 1024], F32, kind="ExternalInput")

    OUT = nc.dram_tensor("OUT", [NS, 128, D], F32, kind="ExternalOutput")

    def scratch(name, tapname):
        kind = "ExternalOutput" if tapname in taps else "Internal"
        return nc.dram_tensor(name, [NS, 128, D], F32, kind=kind)

    SCR_ATT = scratch("SCR_ATT", "attnout")
    SCR_Y = scratch("SCR_Y", "y")
    SCR_ZP = scratch("SCR_ZP", "zp")
    SCR_Z = scratch("SCR_Z", "z")

    taps_out = {}
    def tap(name, shape, dtype):
        if name in taps:
            taps_out[name] = nc.dram_tensor("tap_" + name, shape, dtype,
                                            kind="ExternalOutput")
            return taps_out[name]
        return None

    t_ln1fm = tap("ln1fm", [128, KC * 1024], BF16)
    t_kf = tap("kf", [128, S2], BF16)
    t_vt = tap("vt", [128, NT * 128], BF16)
    t_q0 = tap("q0", [128, 1024], BF16)
    t_wq0 = tap("wq0", [128, KC * 128], BF16)
    t_afm = tap("afm", [128, HN * 1024], BF16)
    t_ln2fm = tap("ln2fm", [128, KC * 1024], BF16)
    t_route = tap("route", [128, NS], F32)
    t_xlat = tap("xlat", [128, LC * 1024], BF16)
    t_eos = tap("eos", [128, LC * 1024], BF16)
    t_h1 = tap("h1", [128, 32 * 1024], BF16)

    with ExitStack() as st:
        SEM = lambda n: Sem(st.enter_context(nc.semaphore(n)))

        def SBL(n, s, d):
            return st.enter_context(nc.sbuf_tensor(n, s, d, side="left"))
        WSB = SBL("WSB", [128, 2, 4096], BF16)
        IDB = SBL("IDB", [128, 128], BF16)
        MA = SBL("MA", [128, 128], F32)
        MB = SBL("MB", [128, 128], F32)
        GTB = SBL("GTB", [128, KC * NE], BF16)
        A = SBL("A", [128, KC * 1024], BF16)
        B = SBL("B", [128, KC * 1024], BF16)
        SML = SBL("SML", [128, 40], F32)

        def sml(i):
            return SML[:, i:i + 1]

        AV32 = A[:].bitcast(F32)
        BV32 = B[:].bitcast(F32)

        sems = {}
        for name in ("cXL cW sSQ1 sVAR sRSTD sLN sTP sLNF sKV sKVE sVT sVTE "
                     "sP1 sQP sQF sSCB sMX sEXP sPT sPRT sAV sAVE sATP sAFM "
                     "sWOJ sWOE cSC sP4L sP4X sT1 sY cY sLN2 sTP2 sLNF2 sP4 "
                     "sGL sRT sF1 sF1E sEX sEXE sTPE sEOS sUP sREL sDN sZE "
                     "cZP cZPL sZE2 cZ sWR2 cEXA sIA sTG sWW sUN sO cO cT "
                     "sGB cG sGBm sGBe sQB sRCP sGLE sM1 sG2 sM2 sM1B sEQ sG2B sM2R cXLa cXLb cQa cQb cBa cBb cAa cAb cXa cXb cYa cYb cZPa cZPb cZLa cZLb cZa cZb cOa cOb cS0 cS1 cS2 cS3").split():
            sems[name] = SEM(name)
        g = sems
        cXL, cW = g["cXL"], g["cW"]
        sSQ1, sVAR, sRSTD, sLN = g["sSQ1"], g["sVAR"], g["sRSTD"], g["sLN"]
        sTP, sLNF, sKV, sKVE = g["sTP"], g["sLNF"], g["sKV"], g["sKVE"]
        sVT, sVTE, sP1 = g["sVT"], g["sVTE"], g["sP1"]
        sQP, sQF, sSCB, sMX = g["sQP"], g["sQF"], g["sSCB"], g["sMX"]
        sEXP, sPT, sPRT, sAV = g["sEXP"], g["sPT"], g["sPRT"], g["sAV"]
        sAVE, sATP, sAFM = g["sAVE"], g["sATP"], g["sAFM"]
        sWOJ, sWOE, cSC = g["sWOJ"], g["sWOE"], g["cSC"]
        sP4L, sP4X, sT1, sY, cY = g["sP4L"], g["sP4X"], g["sT1"], g["sY"], g["cY"]
        sLN2, sTP2, sLNF2, sP4 = g["sLN2"], g["sTP2"], g["sLNF2"], g["sP4"]
        sGL, sRT, sF1, sF1E = g["sGL"], g["sRT"], g["sF1"], g["sF1E"]
        sEX, sEXE, sTPE, sEOS = g["sEX"], g["sEXE"], g["sTPE"], g["sEOS"]
        sUP, sREL, sDN, sZE = g["sUP"], g["sREL"], g["sDN"], g["sZE"]
        cZP, cZPL, sZE2, cZ = g["cZP"], g["cZPL"], g["sZE2"], g["cZ"]
        sWR2, cEXA = g["sWR2"], g["cEXA"]
        sIA, sTG, sWW, sUN, sO, cO, cT = (g["sIA"], g["sTG"], g["sWW"],
                                          g["sUN"], g["sO"], g["cO"], g["cT"])
        sGB = g["sGB"]
        cG, sGBm, sGBe, sQB = g["cG"], g["sGBm"], g["sGBe"], g["sQB"]
        sRCP, sGLE, sM1, sG2, sM2 = (g["sRCP"], g["sGLE"], g["sM1"],
                                     g["sG2"], g["sM2"])
        sM1B, sEQ, sG2B, sM2R = (g["sM1B"], g["sEQ"], g["sG2B"],
                                 g["sM2R"])
        cXL2 = [g["cXLa"], g["cXLb"]]
        cQ2 = [g["cQa"], g["cQb"]]
        cB2 = [g["cBa"], g["cBb"]]
        cA2 = [g["cAa"], g["cAb"]]
        cX2 = [g["cXa"], g["cXb"]]
        cY2 = [g["cYa"], g["cYb"]]
        cZP2 = [g["cZPa"], g["cZPb"]]
        cZL2 = [g["cZLa"], g["cZLb"]]
        cZ2 = [g["cZa"], g["cZb"]]
        cO2 = [g["cOa"], g["cOb"]]
        cS4 = [g["cS0"], g["cS1"], g["cS2"], g["cS3"]]

        def ln_dve_sum(v, k, srcap):
            p = k % 2
            sVAR.inc(v.tensor_reduce(out=sml(0 + p), in_=srcap, axis=AX.X,
                                     op=OP.add))

        def ln_dve_var(v, k):
            p = k % 2
            v.wait_ge(sSQ1.h, k + 1)
            sVAR.inc(v.tensor_scalar(out=sml(8 + p), in0=sml(2 + p),
                                     scalar1=1.0 / D, scalar2=sml(6 + p),
                                     op0=OP.mult, op1=OP.subtract))

        def ln_act(s, k, srcap, junk):
            p = k % 2
            s.activation(out=junk, in_=srcap, func=AF.Square,
                         accum_out=sml(2 + p))
            s.wait_ge(sVAR.h, 2 * k + 1)
            s.activation(out=sml(4 + p), in_=sml(0 + p), func=AF.Identity,
                         scale=-1.0 / D)
            sSQ1.inc(s.activation(out=sml(6 + p), in_=sml(4 + p),
                                  func=AF.Square))
            s.wait_ge(sVAR.h, 2 * k + 2)
            s.activation(out=sml(10 + p), in_=sml(8 + p), func=AF.Ln,
                         bias=sml(39))
            s.activation(out=sml(12 + p), in_=sml(10 + p), func=AF.Exp,
                         scale=-0.5)
            sRSTD.inc(s.activation(out=sml(14 + p), in_=sml(4 + p),
                                   func=AF.Identity, scale=sml(12 + p)))

        wsb_sched = []
        wsb_release = {}
        qg_sched = []

        def wsb_push(src_ap, kdim):
            wsb_sched.append((src_ap, kdim))
            qg_sched.append(("wsb", len(wsb_sched) - 1))
            return len(wsb_sched) - 1

        def qg_raw_push(dst_ap, src_ap, sem, waits):
            qg_sched.append(("raw", (dst_ap, src_ap, sem, list(waits))))

        def wsb_view(i, kdim):
            return WSB[:, i % 2].rearrange("p (k c) -> p k c", k=kdim)

        block = st.enter_context(nc.Block())

        p1 = ExitStack()
        def SBR1(n, s, d):
            return p1.enter_context(nc.sbuf_tensor(n, s, d, side="right"))
        KF = SBR1("KF", [128, S2], BF16)
        VT = SBR1("VT", [128, NT, 128], BF16)
        QFB = SBR1("QFB", [128, 2, 1024], BF16)
        KL2F = SBR1("KL2F", [2, S2], FP16)
        Q2S = SBR1("Q2S", [2, 1024], F32)
        p1tmp = ExitStack()
        def SBR1T(n, s, d):
            return p1tmp.enter_context(nc.sbuf_tensor(n, s, d, side="right"))
        WKVB = SBR1T("WKVB", [128, 2, KC, 128], BF16)
        LNF = SBR1T("LNF", [128, D], BF16)
        VSB = SBR1T("VSB", [128, 128], BF16)

        XT = [BV32[:, k * 4096:(k + 1) * 4096] for k in range(2)]
        LNT = [B[:, 16384 + k * 4096: 16384 + (k + 1) * 4096] for k in range(2)]
        SQJ = BV32[:, 12288:16384]

        p1ps = ExitStack()
        TPP = [p1ps.enter_context(nc.psum_tensor(f"TP{k}", [128, 1024], BF16))
               for k in range(2)]
        KVK = p1ps.enter_context(nc.psum_tensor("KVK", [128, 128], F32))
        KVV = p1ps.enter_context(nc.psum_tensor("KVV", [128, 128], F32))
        VTP = p1ps.enter_context(nc.psum_tensor("VTP", [128, 128], BF16))

        @block.gpsimd
        def _(q):
            cW.inc(q.dma_start(out=MA[:], in_=MTRI[:]), 16)
            cW.inc(q.dma_start(out=MB[:], in_=MROL[:]), 16)
            cW.inc(q.dma_start(out=KL2F[:], in_=KL2[:]), 16)
            cW.inc(q.dma_start(out=Q2S[:], in_=Q2[:]), 16)
            cW.inc(q.dma_start(out=GTB[:], in_=GTH[:]), 16)
            for kv in range(2):
                cW.inc(q.dma_start(out=WKVB[:, kv],
                                   in_=WKV4[kv].rearrange("k p j -> p k j")), 16)
            cW.inc(q.dma_start(out=IDB[:], in_=IDBH[:]), 16)
        CW_CONST = cW.n

        @block.sync
        def _(q):
            for t in range(NT):
                if t >= 2:
                    q.wait_ge(sLN.h, t - 1)
                    q.wait_ge(sSQ1.h, t - 1)
                cXL2[t % 2].inc(q.dma_start(out=XT[t % 2], in_=XB[t]), 16)

        @block.scalar
        def _(s):
            for t in range(NT):
                s.wait_ge(cXL2[t % 2].h, 16 * (t // 2 + 1))
                ln_act(s, t, XT[t % 2], SQJ)

        @block.vector
        def _(v):
            v.memset(sml(39), EPS)
            for t in range(NT):
                par = t % 2
                v.wait_ge(cXL2[par].h, 16 * (t // 2 + 1))
                ln_dve_sum(v, t, XT[par])
                ln_dve_var(v, t)
                v.wait_ge(sRSTD.h, t + 1)
                sLN.inc(v.tensor_scalar(out=LNT[par], in0=XT[par],
                                        scalar1=sml(12 + par),
                                        scalar2=sml(14 + par),
                                        op0=OP.mult, op1=OP.add))
                for gi in range(4):
                    v.wait_ge(sTP.h, 4 * t + gi + 1)
                    if t % 2 == 0:
                        so = t // 2
                        dstap = A[:].rearrange("p (k c) -> p k c", k=KC)[
                            :, gi * 8:(gi + 1) * 8, so * 128:so * 128 + 128]
                    else:
                        dstap = LNF[:].rearrange("p (k c) -> p k c", k=KC)[
                            :, gi * 8:(gi + 1) * 8, 0:128]
                    srcap = TPP[gi % 2][:].rearrange("p (k c) -> p k c", k=8)
                    sLNF.inc(v.tensor_copy(dstap, srcap))
                v.wait_ge(sKV.h, 2 * t + 1)
                sKVE.inc(v.tensor_copy(KF[:, t * 128:(t + 1) * 128], KVK[:]))
                v.wait_ge(sKV.h, 2 * t + 2)
                sKVE.inc(v.tensor_copy(VSB[:], KVV[:]))
                v.wait_ge(sVT.h, t + 1)
                sVTE.inc(v.tensor_copy(VT[:, t, :], VTP[:]))
            v.wait_ge(sLNF.h, 4 * NT)
            sP1.inc(v.memset(sml(38), 0.0))

        @block.tensor
        def _(te):
            te.wait_ge(cW.h, CW_CONST)
            for t in range(NT):
                par = t % 2
                te.wait_ge(sLN.h, t + 1)
                for gi in range(4):
                    gidx = 4 * t + gi
                    if gidx >= 2:
                        te.wait_ge(sLNF.h, gidx - 1)
                    for k8 in range(8):
                        kc = gi * 8 + k8
                        tr = te.transpose(TPP[gi % 2][:, k8 * 128:(k8 + 1) * 128],
                                          LNT[par][:, kc * 128:(kc + 1) * 128],
                                          IDB[:])
                    sTP.inc(tr)
                te.wait_ge(sLNF.h, 4 * t + 4)
                if t >= 1:
                    te.wait_ge(sKVE.h, 2 * t)
                if t % 2 == 0:
                    so = t // 2
                    rview = A[:].rearrange("p (k c) -> p k c", k=KC)[
                        :, :, so * 128:so * 128 + 128]
                else:
                    rview = LNF[:].rearrange("p (k c) -> p k c", k=KC)
                for kv, pdst in ((0, KVK), (1, KVV)):
                    for kc in range(KC):
                        mm = te.matmul(pdst[:], WKVB[:, kv, kc], rview[:, kc],
                                       start=(kc == 0), stop=(kc == KC - 1))
                    sKV.inc(mm)
                te.wait_ge(sKVE.h, 2 * t + 2)
                if t >= 1:
                    te.wait_ge(sVTE.h, t)
                sVT.inc(te.transpose(VTP[:], VSB[:], IDB[:]))

        if t_ln1fm is not None:
            @block.sync
            def _(q):
                q.wait_ge(sP1.h, 1)
                cT.inc(q.dma_start(out=t_ln1fm[:], in_=A[:]), 16)
        if t_kf is not None:
            @block.sync
            def _(q):
                q.wait_ge(sP1.h, 1)
                cT.inc(q.dma_start(out=t_kf[:], in_=KF[:]), 16)
        if t_vt is not None:
            @block.sync
            def _(q):
                q.wait_ge(sP1.h, 1)
                cT.inc(q.dma_start(out=t_vt[:],
                                   in_=VT[:].rearrange("p t c -> p (t c)")), 16)

        if upto >= 1:
            p1tmp.close()
            p2 = ExitStack()
            def SBR2(n, s, d):
                return p2.enter_context(nc.sbuf_tensor(n, s, d, side="right"))
            PR = SBR2("PR", [128, S2], BF16)
            PRT = SBR2("PRT", [128, S2], BF16)
            WQB = SBR2("WQB", [128, 2, KC, 128], BF16)
            QB2 = SBR2("QB2", [2, 2, 1024], FP16)
            ATTB = SBR2("ATTB", [128, 2, 128], BF16)

            p1ps.close()
            p2ps = ExitStack()
            SC = p2ps.enter_context(nc.psum_tensor("SC", [128, 2048], F32))
            PTP = p2ps.enter_context(nc.psum_tensor("PTP", [128, 1024], BF16))
            AVP = p2ps.enter_context(nc.psum_tensor("AVP", [128, 128], F32))
            ATP = p2ps.enter_context(nc.psum_tensor("ATP", [128, 128], BF16))
            QPS = p2ps.enter_context(nc.psum_tensor("QPS", [128, 512], F32))

            def nblk(sl):
                return ((2 * sl + 2) * 128 + 511) // 512

            def bw(sl, j):
                return min(512, (2 * sl + 2) * 128 - j * 512)

            NBLK_H = sum(nblk(x) for x in range(NS))
            NPTG_S = [(2 * x + 2 + 7) // 8 for x in range(NS)]
            NPTG_H = sum(NPTG_S)

            @block.gpsimd
            def _(q):
                for h in range(HN):
                    if h == 0:
                        q.wait_ge(sP1.h, 1)
                    if h >= 2:
                        q.wait_ge(sQP.h, 2 * (h - 1))
                        if h == 2 and (t_q0 is not None or t_wq0 is not None):
                            q.wait_ge(cT.h, 16 * ((t_q0 is not None)
                                                  + (t_wq0 is not None)))
                    cQ2[h % 2].inc(
                        q.dma_start(out=WQB[:, h % 2],
                                    in_=WQ4[h].rearrange("k p j -> p k j")),
                        16)

            @block.scalar
            def _(s):
                s.wait_ge(cW.h, CW_CONST)
                for h in range(HN):
                    if h == 0:
                        s.wait_ge(sP1.h, 1)
                    if h >= 2:
                        s.wait_ge(sSCB.h, h * NBLK_H - NBLK_H + 1)
                    sQB.inc(s.activation(out=QB2[:, h % 2], in_=Q2S[:],
                                         func=AF.Identity, scale=_slope(h)))
                    for sl in range(NS):
                        it = h * NS + sl
                        par = it % 2
                        ka = (2 * sl + 2) * 128
                        s.wait_ge(sMX.h, it + 1)
                        sEXP.inc(s.activation(out=PR[:, 0:ka], in_=SC[:, 0:ka],
                                              func=AF.Exp,
                                              bias=sml(16 + par),
                                              accum_out=sml(18 + par)))
                        base_g = h * NPTG_H + sum(NPTG_S[:sl])
                        for gi in range(NPTG_S[sl]):
                            n128 = min(8, 2 * sl + 2 - gi * 8)
                            s.wait_ge(sPT.h, base_g + gi + 1)
                            sPRT.inc(s.activation(
                                out=PRT[:, gi * 1024:gi * 1024 + n128 * 128],
                                in_=PTP[:, 0:n128 * 128], func=AF.Copy))
                        s.wait_ge(sAV.h, it + 1)
                        s.wait_ge(sRCP.h, it + 1)
                        if it >= 2:
                            s.wait_ge(sATP.h, it - 1)
                        sAVE.inc(s.activation(out=ATTB[:, par], in_=AVP[:],
                                              func=AF.Identity,
                                              scale=sml(20 + par)))
                        s.wait_ge(sATP.h, it + 1)
                        sAFM.inc(s.activation(
                            out=B[:, h * 1024 + sl * 128:
                                 h * 1024 + (sl + 1) * 128],
                            in_=ATP[:], func=AF.Copy))

            @block.vector
            def _(v):
                for h in range(HN):
                    for half in range(2):
                        v.wait_ge(sQP.h, 2 * h + half + 1)
                        sQF.inc(v.tensor_scalar(
                            out=QFB[:, h % 2, half * 512:(half + 1) * 512],
                            in0=QPS[:], scalar1=SCALE, scalar2=None,
                            op0=OP.mult, op1=OP.bypass))
                    for sl in range(NS):
                        it = h * NS + sl
                        par = it % 2
                        nb = nblk(sl)
                        base_b = h * NBLK_H + sum(nblk(x) for x in range(sl))
                        ka = (2 * sl + 2) * 128
                        v.wait_ge(sSCB.h, base_b + nb)
                        v.tensor_tensor(out=SC[:, ka - 256:ka - 128],
                                        in0=SC[:, ka - 256:ka - 128],
                                        in1=MA[:], op=OP.add)
                        v.tensor_tensor(out=SC[:, ka - 128:ka],
                                        in0=SC[:, ka - 128:ka],
                                        in1=MB[:], op=OP.add)
                        sMX.inc(v.tensor_reduce(out=sml(16 + par),
                                                in_=SC[:, 0:ka],
                                                axis=AX.X, op=OP.max,
                                                negate=True))
                        v.wait_ge(sEXP.h, it + 1)
                        sRCP.inc(v.reciprocal(sml(20 + par), sml(18 + par)))

            @block.tensor
            def _(te):
                te.wait_ge(sP1.h, 1)
                for h in range(HN):
                    te.wait_ge(cQ2[h % 2].h, 16 * (h // 2 + 1))
                    for half in range(2):
                        if 2 * h + half >= 1:
                            te.wait_ge(sQF.h, 2 * h + half)
                        for kc in range(KC):
                            mm = te.matmul(
                                QPS[:], WQB[:, h % 2, kc],
                                A[:, kc * 1024 + half * 512:
                                  kc * 1024 + (half + 1) * 512],
                                start=(kc == 0), stop=(kc == KC - 1))
                        sQP.inc(mm)
                    for sl in range(NS):
                        it = h * NS + sl
                        nb = nblk(sl)
                        te.wait_ge(sQF.h, 2 * h + 2)
                        te.wait_ge(sQB.h, h + 1)
                        if it > 0:
                            te.wait_ge(sEXP.h, it)
                        for j in range(nb):
                            te.matmul(SC[:, j * 512:j * 512 + bw(sl, j)],
                                      QFB[:, h % 2, sl * 128:(sl + 1) * 128],
                                      KF[:, j * 512:j * 512 + bw(sl, j)],
                                      start=True, stop=False)
                            mm = te.matmul(SC[:, j * 512:j * 512 + bw(sl, j)],
                                           QB2[:, h % 2, sl * 128:(sl + 1) * 128],
                                           KL2F[:, j * 512:j * 512 + bw(sl, j)],
                                           start=False, stop=True)
                            sSCB.inc(mm)
                        te.wait_ge(sEXP.h, it + 1)
                        base_g = h * NPTG_H + sum(NPTG_S[:sl])
                        for gi in range(NPTG_S[sl]):
                            if base_g + gi >= 1:
                                te.wait_ge(sPRT.h, base_g + gi)
                            n128 = min(8, 2 * sl + 2 - gi * 8)
                            for k8 in range(n128):
                                kb = gi * 8 + k8
                                tr = te.transpose(
                                    PTP[:, k8 * 128:(k8 + 1) * 128],
                                    PR[:, kb * 128:(kb + 1) * 128], IDB[:])
                            sPT.inc(tr)
                        if it > 0:
                            te.wait_ge(sAVE.h, it)
                        for gi in range(NPTG_S[sl]):
                            te.wait_ge(sPRT.h, base_g + gi + 1)
                            n128 = min(8, 2 * sl + 2 - gi * 8)
                            for k8 in range(n128):
                                kb = gi * 8 + k8
                                mm = te.matmul(AVP[:],
                                               PRT[:, gi * 1024 + k8 * 128:
                                                   gi * 1024 + (k8 + 1) * 128],
                                               VT[:, kb, :],
                                               start=(kb == 0),
                                               stop=(kb == 2 * sl + 1))
                        sAV.inc(mm)
                        te.wait_ge(sAVE.h, it + 1)
                        if it > 0:
                            te.wait_ge(sAFM.h, it)
                        sATP.inc(te.transpose(ATP[:], ATTB[:, it % 2], IDB[:]))

            if t_q0 is not None:
                @block.sync
                def _(q):
                    q.wait_ge(sQF.h, 2)
                    cT.inc(q.dma_start(out=t_q0[:], in_=QFB[:, 0]), 16)
            if t_wq0 is not None:
                @block.sync
                def _(q):
                    q.wait_ge(cW.h, CW_CONST + 16)
                    cT.inc(q.dma_start(
                        out=t_wq0[:],
                        in_=WQB[:, 0].rearrange("p k j -> p (k j)")), 16)
            if t_afm is not None:
                @block.sync
                def _(q):
                    q.wait_ge(sAFM.h, HN * NS)
                    cT.inc(q.dma_start(out=t_afm[:], in_=B[:]), 16)

        if upto >= 2:
            p2ps.close()
            p3 = ExitStack()
            ATSTG = p3.enter_context(
                nc.sbuf_tensor("ATSTG", [128, 4, 512], F32, side="right"))
            p3ps = ExitStack()
            WOP = p3ps.enter_context(nc.psum_tensor("WOP", [128, 4096], F32))

            wo_blk0 = len(wsb_sched)
            for jb in range(8):
                for qt in range(4):
                    wsb_push(WO4[jb, qt * 8:(qt + 1) * 8]
                             .rearrange("k p j -> p k j"), 8)

            wo_done = {}

            @block.tensor
            def _(te):
                te.wait_ge(sAFM.h, HN * NS)
                for jb in range(8):
                    for qt in range(4):
                        i = wo_blk0 + jb * 4 + qt
                        te.wait_ge(cB2[i % 2].h, 16 * (i // 2 + 1))
                        if qt == 0 and jb > 0:
                            te.wait_ge(sWOE.h, 8 * jb)
                        for k8 in range(8):
                            kc = qt * 8 + k8
                            for t in range(8):
                                mm = te.matmul(
                                    WOP[:, t * 512:(t + 1) * 512],
                                    B[:, kc * 1024 + t * 128:
                                      kc * 1024 + (t + 1) * 128],
                                    wsb_view(i, 8)[:, k8],
                                    start=(kc == 0), stop=(kc == KC - 1))
                        sWR2.inc(mm)
                        wsb_release[i] = (sWR2.h, sWR2.n)
                    wo_done[jb] = sWR2.n

            @block.scalar
            def _(s):
                for jb in range(8):
                    s.wait_ge(sWR2.h, wo_done[jb])
                    for t in range(8):
                        idx = 8 * jb + t
                        if idx >= 4:
                            s.wait_ge(cS4[t % 4].h, 16 * (idx // 4))
                        sWOE.inc(s.activation(out=ATSTG[:, t % 4],
                                              in_=WOP[:, t * 512:(t + 1) * 512],
                                              func=AF.Copy))

            @block.sync
            def _(q):
                for jb in range(8):
                    for t in range(8):
                        q.wait_ge(sWOE.h, 8 * jb + t + 1)
                        cS4[t % 4].inc(q.dma_start(
                            out=SCR_ATT[t][:, jb * 512:(jb + 1) * 512],
                            in_=ATSTG[:, t % 4]), 16)

        if upto >= 3:
            p3ps.close()
            p3.close()
            p2.close()
            p1.close()
            p4 = ExitStack()
            LNT2 = p4.enter_context(
                nc.sbuf_tensor("LNT2", [128, 2, D], BF16, side="right"))
            SQJ4 = p4.enter_context(
                nc.sbuf_tensor("SQJ4", [128, D], F32, side="right"))
            p4ps = ExitStack()
            TP2 = [p4ps.enter_context(nc.psum_tensor(f"TP2{k}", [128, 1024],
                                                     BF16)) for k in range(2)]
            ATO = [AV32[:, k * 4096:(k + 1) * 4096] for k in range(2)]
            XSL = [AV32[:, (2 + k) * 4096:(3 + k) * 4096] for k in range(2)]

            @block.sync
            def _(q):
                q.wait_ge(sQP.h, 2 * HN)
                for ss in range(4):
                    q.wait_ge(cS4[ss].h, 16 * 16)
                for sl in range(NS):
                    if sl >= 2:
                        q.wait_ge(sY.h, sl - 1)
                    cA2[sl % 2].inc(
                        q.dma_start(out=ATO[sl % 2], in_=SCR_ATT[sl]), 16)
                    if sl >= 2:
                        q.wait_ge(sLN2.h, sl - 1)
                        q.wait_ge(cY2[sl % 2].h, 16 * (sl // 2))
                    cX2[sl % 2].inc(
                        q.dma_start(out=XSL[sl % 2], in_=XB[2 * sl]), 16)
                    if sl >= 1:
                        q.wait_ge(sY.h, sl)
                        cY2[(sl - 1) % 2].inc(
                            q.dma_start(out=SCR_Y[sl - 1],
                                        in_=XSL[(sl - 1) % 2]), 16)
                q.wait_ge(sY.h, NS)
                cY2[(NS - 1) % 2].inc(
                    q.dma_start(out=SCR_Y[NS - 1], in_=XSL[(NS - 1) % 2]), 16)

            @block.scalar
            def _(s):
                s.wait_ge(sQP.h, 2 * HN)
                for ss in range(4):
                    s.wait_ge(cS4[ss].h, 16 * 16)
                for sl in range(NS):
                    par = sl % 2
                    kato, ky = 16 + 2 * sl, 17 + 2 * sl
                    s.wait_ge(cA2[par].h, 16 * (sl // 2 + 1))
                    ln_act(s, kato, ATO[par], SQJ4[:])
                    s.wait_ge(sY.h, sl + 1)
                    ln_act(s, ky, XSL[par], SQJ4[:])

            @block.vector
            def _(v):
                for sl in range(NS):
                    par = sl % 2
                    kato, ky = 16 + 2 * sl, 17 + 2 * sl
                    v.wait_ge(cA2[par].h, 16 * (sl // 2 + 1))
                    ln_dve_sum(v, kato, ATO[par])
                    ln_dve_var(v, kato)
                    v.wait_ge(sRSTD.h, kato + 1)
                    sT1.inc(v.tensor_scalar(out=ATO[par], in0=ATO[par],
                                            scalar1=sml(12 + kato % 2),
                                            scalar2=sml(14 + kato % 2),
                                            op0=OP.mult, op1=OP.add))
                    v.wait_ge(cX2[par].h, 16 * (sl // 2 + 1))
                    sY.inc(v.tensor_tensor(out=XSL[par], in0=XSL[par],
                                           in1=ATO[par], op=OP.add))
                    ln_dve_sum(v, ky, XSL[par])
                    ln_dve_var(v, ky)
                    v.wait_ge(sRSTD.h, ky + 1)
                    sLN2.inc(v.tensor_scalar(out=LNT2[:, par], in0=XSL[par],
                                             scalar1=sml(12 + ky % 2),
                                             scalar2=sml(14 + ky % 2),
                                             op0=OP.mult, op1=OP.add))
                    for gi in range(4):
                        if SKIP_LN2T:
                            sLNF2.inc(v.memset(sml(38), 0.0))
                            continue
                        v.wait_ge(sTP2.h, 4 * sl + gi + 1)
                        dstap = B[:].rearrange("p (k c) -> p k c", k=KC)[
                            :, gi * 8:(gi + 1) * 8, sl * 128:sl * 128 + 128]
                        srcap = TP2[gi % 2][:].rearrange("p (k c) -> p k c", k=8)
                        sLNF2.inc(v.tensor_copy(dstap, srcap))
                sP4.inc(v.memset(sml(38), 0.0))



            @block.tensor
            def _(te):
                for sl in range(NS):
                    par = sl % 2
                    te.wait_ge(sLN2.h, sl + 1)
                    for gi in range(4):
                        gidx = 4 * sl + gi
                        if SKIP_LN2T:
                            sTP2.inc(te.nop())
                            continue
                        if gidx >= 2:
                            te.wait_ge(sLNF2.h, gidx - 1)
                        for k8 in range(8):
                            kc = gi * 8 + k8
                            tr = te.transpose(
                                TP2[gi % 2][:, k8 * 128:(k8 + 1) * 128],
                                LNT2[:, par, kc * 128:(kc + 1) * 128], IDB[:])
                        sTP2.inc(tr)

            if t_ln2fm is not None:
                @block.sync
                def _(q):
                    q.wait_ge(sP4.h, 1)
                    cT.inc(q.dma_start(out=t_ln2fm[:], in_=B[:]), 16)

        if upto >= 4:
            p4ps.close()
            p4.close()
            p45 = ExitStack()
            def SBR45(n, s, d):
                return p45.enter_context(nc.sbuf_tensor(n, s, d, side="right"))
            C = SBR45("C", [128, LC * 1024], BF16)
            GL = SBR45("GL", [128, NE], F32)
            EQ = SBR45("EQ", [128, NE], F32)
            G2 = SBR45("G2", [128, NE], F32)
            ROUTE = SBR45("ROUTE", [128, NS], F32)
            EOTM = A[:, 0:NS * 2048]
            EXA = A[:, 16384:16384 + LC * 512].rearrange("p (k c) -> p k c",
                                                         k=LC)

            p45ps = ExitStack()
            GPS = p45ps.enter_context(nc.psum_tensor("GPS", [128, NE], F32))
            F1PS = [p45ps.enter_context(
                nc.psum_tensor(f"F1PS{k}", [128, 1024], F32)) for k in range(2)]
            EXPS = [p45ps.enter_context(
                nc.psum_tensor(f"EXPS{k}", [128, 512], F32)) for k in range(2)]
            TPE = p45ps.enter_context(nc.psum_tensor("TPE", [128, 1024], BF16))

            f1_blk0 = len(wsb_sched)
            for lb in range(LC):
                wsb_push(F14[lb].rearrange("k p j -> p k j"), KC)
            exw_blk0 = len(wsb_sched)
            if P45_PARTS == "full":
                for hf in range(2):
                    wsb_push(EX4[1, hf * 8:(hf + 1) * 8]
                             .rearrange("k p j -> p k j"), 8)

            if P45_PARTS == "full":
                qg_raw_push(EXA[:, 0:8],
                            EX4[0, 0:8].rearrange("k p j -> p k j"), cEXA,
                            [(cY2[0].h, 16 * (NS // 2)),
                             (cY2[1].h, 16 * (NS // 2)), (sLN2.h, NS)])
                qg_raw_push(EXA[:, 8:16],
                            EX4[0, 8:16].rearrange("k p j -> p k j"), cEXA, [])
                qg_raw_push(EXA[:, 0:8],
                            EX4[2, 0:8].rearrange("k p j -> p k j"), cEXA,
                            [(sEX.h, 16)])
                qg_raw_push(EXA[:, 8:16],
                            EX4[2, 8:16].rearrange("k p j -> p k j"), cEXA, [])
                for hf in range(2):
                    wsb_push(EX4[3, hf * 8:(hf + 1) * 8]
                             .rearrange("k p j -> p k j"), 8)

            @block.tensor
            def _(te):
                te.wait_ge(sP4.h, 1)
                for sl in range(NS):
                    if P45_PARTS == "fc1":
                        break
                    if sl > 0:
                        te.wait_ge(sRT.h, sl)
                    for kc in range(KC):
                        mm = te.matmul(GPS[:],
                                       B[:, kc * 1024 + sl * 128:
                                         kc * 1024 + (sl + 1) * 128],
                                       GTB[:, kc * NE:(kc + 1) * NE],
                                       start=(kc == 0), stop=(kc == KC - 1))
                    sGL.inc(mm)
                for lb in range(LC):
                    i = f1_blk0 + lb
                    te.wait_ge(cB2[i % 2].h, 16 * (i // 2 + 1))
                    if lb >= 2:
                        te.wait_ge(sF1E.h, 2 * (lb - 1))
                    for half in range(2):
                        for kc in range(KC):
                            mm = te.matmul(
                                F1PS[lb % 2][:, half * 512:(half + 1) * 512],
                                wsb_view(i, KC)[:, kc],
                                B[:, kc * 1024 + half * 512:
                                  kc * 1024 + (half + 1) * 512],
                                start=(kc == 0), stop=(kc == KC - 1))
                        sF1.inc(mm)
                    wsb_release[i] = (sF1.h, sF1.n)
                te.wait_ge(sF1E.h, 2 * LC)
                for mb in range(4 if P45_PARTS == "full" else 0):
                    if mb % 2 == 0:
                        te.wait_ge(cEXA.h, 32 * (mb // 2 + 1))
                    else:
                        iA = exw_blk0 + (mb // 2) * 2
                        te.wait_ge(cB2[iA % 2].h, 16 * (iA // 2 + 1))
                        te.wait_ge(cB2[(iA + 1) % 2].h, 16 * ((iA + 1) // 2 + 1))
                    for t in range(NS):
                        ei = mb * NS + t
                        if ei >= 2:
                            te.wait_ge(sEXE.h, ei - 1)
                        for lc in range(LC):
                            if mb % 2 == 0:
                                rhs = EXA[:, lc, :]
                            else:
                                iblk = exw_blk0 + (mb // 2) * 2 + lc // 8
                                rhs = wsb_view(iblk, 8)[:, lc % 8]
                            mm = te.matmul(EXPS[ei % 2][:],
                                           C[:, lc * 1024 + t * 128:
                                             lc * 1024 + (t + 1) * 128],
                                           rhs,
                                           start=(lc == 0), stop=(lc == LC - 1))
                        sEX.inc(mm)
                    if mb % 2 == 1 and t == NS - 1:
                        iA = exw_blk0 + (mb // 2) * 2
                        wsb_release[iA] = (sEX.h, sEX.n)
                        wsb_release[iA + 1] = (sEX.h, sEX.n)
                if P45_PARTS == "full":
                    te.wait_ge(sEXE.h, 32)
                    gidx = 0
                    for t in range(NS):
                        for gi in range(2):
                            if gidx >= 1:
                                te.wait_ge(sEOS.h, gidx)
                            for k8 in range(8):
                                mc = gi * 8 + k8
                                tr = te.transpose(
                                    TPE[:, k8 * 128:(k8 + 1) * 128],
                                    EOTM[:, t * 2048 + mc * 128:
                                         t * 2048 + (mc + 1) * 128], IDB[:])
                            sTPE.inc(tr)
                            gidx += 1

            @block.scalar
            def _(s):
                for sl in range(NS if P45_PARTS != "fc1" else 0):
                    s.wait_ge(sGL.h, sl + 1)
                    sGLE.inc(s.activation(out=GL[:], in_=GPS[:], func=AF.Copy))
                    s.wait_ge(sM1.h, sl + 1)
                    sM1B.inc(s.activation(out=sml(35), in_=sml(36),
                                          func=AF.Identity))
                    s.wait_ge(sEQ.h, sl + 1)
                    sG2.inc(s.activation(out=EQ[:], in_=EQ[:],
                                         func=AF.Identity, scale=NEGBIG))
                    s.wait_ge(sM2.h, sl + 1)
                    sG2B.inc(s.activation(out=G2[:], in_=G2[:],
                                          func=AF.Identity))
                    s.wait_ge(sM2R.h, sl + 1)
                    sRT.inc(s.activation(out=ROUTE[:, sl:sl + 1],
                                         in_=sml(37), func=AF.Identity,
                                         bias=sml(35)))

            @block.vector
            def _(v):
                for sl in range(NS if P45_PARTS != "fc1" else 0):
                    v.wait_ge(sGLE.h, sl + 1)
                    sM1.inc(v.tensor_reduce(out=sml(36), in_=GL[:], axis=AX.X,
                                            op=OP.max))
                    v.wait_ge(sM1B.h, sl + 1)
                    sEQ.inc(v.tensor_scalar(out=EQ[:], in0=GL[:],
                                            scalar1=sml(35), scalar2=None,
                                            op0=OP.is_equal, op1=OP.bypass))
                    v.wait_ge(sG2.h, sl + 1)
                    sM2.inc(v.tensor_tensor(out=G2[:], in0=GL[:], in1=EQ[:],
                                            op=OP.add))
                    v.wait_ge(sG2B.h, sl + 1)
                    sM2R.inc(v.tensor_reduce(out=sml(37), in_=G2[:],
                                             axis=AX.X, op=OP.max))
                v.wait_ge(cY2[0].h, 16 * (NS // 2))
                v.wait_ge(cY2[1].h, 16 * (NS // 2))
                for mb in range(4 if P45_PARTS == "full" else 0):
                    for t in range(NS):
                        ei = mb * NS + t
                        v.wait_ge(sEX.h, ei + 1)
                        sEXE.inc(v.tensor_scalar(
                            out=EOTM[:, t * 2048 + mb * 512:
                                 t * 2048 + (mb + 1) * 512],
                            in0=EXPS[ei % 2][:], scalar1=ROUTE[:, t:t + 1],
                            scalar2=None, op0=OP.mult, op1=OP.bypass))
                gidx = 0
                for t in range(NS if P45_PARTS == "full" else 0):
                    for gi in range(2):
                        v.wait_ge(sTPE.h, gidx + 1)
                        dstap = C[:].rearrange("p (k c) -> p k c", k=LC)[
                            :, gi * 8:(gi + 1) * 8, t * 128:t * 128 + 128]
                        srcap = TPE[:].rearrange("p (k c) -> p k c", k=8)
                        sEOS.inc(v.tensor_copy(dstap, srcap))
                        gidx += 1

            @block.scalar
            def _(s):
                for lb in range(LC):
                    for half in range(2):
                        s.wait_ge(sF1.h, 2 * lb + half + 1)
                        sF1E.inc(s.activation(
                            out=C[:, lb * 1024 + half * 512:
                                 lb * 1024 + (half + 1) * 512],
                            in_=F1PS[lb % 2][:, half * 512:(half + 1) * 512],
                            func=AF.Copy))

            if t_route is not None:
                @block.sync
                def _(q):
                    q.wait_ge(sRT.h, NS)
                    cT.inc(q.dma_start(out=t_route[:], in_=ROUTE[:]), 16)
            if t_xlat is not None:
                @block.sync
                def _(q):
                    q.wait_ge(sF1E.h, 2 * LC)
                    cT.inc(q.dma_start(out=t_xlat[:], in_=C[:]), 16)
            if t_eos is not None:
                @block.sync
                def _(q):
                    q.wait_ge(sEOS.h, 16)
                    cT.inc(q.dma_start(out=t_eos[:], in_=C[:]), 16)

        if upto >= 5:
            p45ps.close()
            p56 = ExitStack()
            def SBR56(n, s, d):
                return p56.enter_context(nc.sbuf_tensor(n, s, d, side="right"))
            ZSTG = SBR56("ZSTG", [128, 2, 512], F32)
            ZPT = SBR56("ZPT", [128, 2, 512], F32)
            RELT = SBR56("RELT", [128, 2, 512], F32)

            up_blk = {}
            dn_blk = {}
            f2_blk = {}
            for half in range(2):
                for ib in range(32):
                    up_blk[(half, ib)] = len(wsb_sched)
                    wsb_push(UP4[half * 32 + ib].rearrange("k p j -> p k j"), KC)
                for fb in range(8):
                    for qt in range(4):
                        dn_blk[(half, fb, qt)] = len(wsb_sched)
                        wsb_push(DN4[fb, half * 32 + qt * 8:
                                     half * 32 + (qt + 1) * 8]
                                 .rearrange("k p j -> p k j"), 8)
                    if half == 1:
                        for hf in range(2):
                            f2_blk[(fb, hf)] = len(wsb_sched)
                            wsb_push(F24[fb, hf * 8:(hf + 1) * 8]
                                     .rearrange("k p j -> p k j"), 8)

            ct_h1_box = [None]

            p5aps = ExitStack()
            UPSA = [p5aps.enter_context(
                nc.psum_tensor(f"UPSA{k}", [128, 512], F32)) for k in range(4)]

            def emit_up(half, UPS, relu_gate_first):
                @block.tensor
                def _(te):
                    if half == 0:
                        te.wait_ge(sEOS.h, 16)
                        te.wait_ge(sF1E.h, 2 * LC)
                        te.wait_ge(sEXE.h, 32)
                    else:
                        te.wait_ge(sZE.h, 64)
                    for ib in range(32):
                        gi = half * 32 + ib
                        i = up_blk[(half, ib)]
                        te.wait_ge(cB2[i % 2].h, 16 * (i // 2 + 1))
                        if ib >= 2:
                            te.wait_ge(sREL.h, half * 64 + 2 * (ib - 1))
                        for hf in range(2):
                            for kc in range(KC):
                                mm = te.matmul(
                                    UPS[2 * (ib % 2) + hf][:],
                                    wsb_view(i, KC)[:, kc],
                                    B[:, kc * 1024 + hf * 512:
                                      kc * 1024 + (hf + 1) * 512],
                                    start=(kc == 0), stop=(kc == KC - 1))
                            sUP.inc(mm)
                        wsb_release[i] = (sUP.h, sUP.n)

                @block.scalar
                def _(s):
                    if half == 0:
                        s.wait_ge(sEOS.h, 16)
                        s.wait_ge(sEX.h, 32)
                    elif ct_h1_box[0] is not None:
                        s.wait_ge(cT.h, ct_h1_box[0])
                    for ib in range(32):
                        for hf in range(2):
                            k = half * 64 + 2 * ib + hf
                            s.wait_ge(sUP.h, k + 1)
                            s.activation(out=RELT[:, k % 2],
                                         in_=UPS[2 * (ib % 2) + hf][:],
                                         func=AF.Relu)
                            sREL.inc(s.activation(
                                out=A[:, ib * 1024 + hf * 512:
                                     ib * 1024 + (hf + 1) * 512],
                                in_=RELT[:, k % 2], func=AF.Square))

            wsb_sched_base = wo_blk0
            emit_up(0, UPSA, True)

            p5aps.close()
            p6aps = ExitStack()
            ZPSA = p6aps.enter_context(nc.psum_tensor("ZPSA", [128, 4096], F32))

            dn_done_a = {}

            def p6a():
                @block.tensor
                def _(te):
                    te.wait_ge(sREL.h, 64)
                    for fb in range(8):
                        if fb > 0:
                            te.wait_ge(sZE.h, 8 * fb)
                        for qt in range(4):
                            i = dn_blk[(0, fb, qt)]
                            te.wait_ge(cB2[i % 2].h, 16 * (i // 2 + 1))
                            for k8 in range(8):
                                ic = qt * 8 + k8
                                for t in range(8):
                                    mm = te.matmul(
                                        ZPSA[:, t * 512:(t + 1) * 512],
                                        A[:, ic * 1024 + t * 128:
                                          ic * 1024 + (t + 1) * 128],
                                        wsb_view(i, 8)[:, k8],
                                        start=(ic == 0), stop=(ic == 31))
                            sWR2.inc(mm)
                            wsb_release[i] = (sWR2.h, sWR2.n)
                        dn_done_a[fb] = sWR2.n

                @block.scalar
                def _(s):
                    for fb in range(8):
                        s.wait_ge(sWR2.h, dn_done_a[fb])
                        for t in range(8):
                            idx = 8 * fb + t
                            if idx >= 2:
                                s.wait_ge(cZP2[idx % 2].h, 16 * (idx // 2))
                            sZE.inc(s.activation(
                                out=ZSTG[:, idx % 2],
                                in_=ZPSA[:, t * 512:(t + 1) * 512],
                                func=AF.Copy))

                @block.sync
                def _(q):
                    for fb in range(8):
                        for t in range(8):
                            idx = 8 * fb + t
                            q.wait_ge(sZE.h, idx + 1)
                            cZP2[idx % 2].inc(q.dma_start(
                                out=SCR_ZP[t][:, fb * 512:(fb + 1) * 512],
                                in_=ZSTG[:, idx % 2]), 16)
            p6a()

            if t_h1 is not None:
                ct_h1 = cT.n + 16
                @block.sync
                def _(q):
                    q.wait_ge(sREL.h, 64)
                    cT.inc(q.dma_start(out=t_h1[:], in_=A[:]), 16)
                ct_h1_box[0] = ct_h1

            p6aps.close()
            p5bps = ExitStack()
            UPSB = [p5bps.enter_context(
                nc.psum_tensor(f"UPSB{k}", [128, 512], F32)) for k in range(4)]
            emit_up(1, UPSB, False)

            p5bps.close()
            p6bps = ExitStack()
            ZPSB = p6bps.enter_context(nc.psum_tensor("ZPSB", [128, 4096], F32))

            dn_done_b = {}

            @block.tensor
            def _(te):
                te.wait_ge(sREL.h, 128)
                for fb in range(8):
                    if fb > 0:
                        te.wait_ge(sZE2.h, 8 * fb)
                    for qt in range(4):
                        i = dn_blk[(1, fb, qt)]
                        te.wait_ge(cB2[i % 2].h, 16 * (i // 2 + 1))
                        for k8 in range(8):
                            ic = qt * 8 + k8
                            for t in range(8):
                                mm = te.matmul(
                                    ZPSB[:, t * 512:(t + 1) * 512],
                                    A[:, ic * 1024 + t * 128:
                                      ic * 1024 + (t + 1) * 128],
                                    wsb_view(i, 8)[:, k8],
                                    start=(ic == 0), stop=False)
                        sWR2.inc(mm)
                        wsb_release[i] = (sWR2.h, sWR2.n)
                    for hf in range(2):
                        i = f2_blk[(fb, hf)]
                        te.wait_ge(cB2[i % 2].h, 16 * (i // 2 + 1))
                        for k8 in range(8):
                            lc = hf * 8 + k8
                            for t in range(8):
                                mm = te.matmul(
                                    ZPSB[:, t * 512:(t + 1) * 512],
                                    C[:, lc * 1024 + t * 128:
                                      lc * 1024 + (t + 1) * 128],
                                    wsb_view(i, 8)[:, k8],
                                    start=False, stop=(lc == LC - 1))
                        sWR2.inc(mm)
                        wsb_release[i] = (sWR2.h, sWR2.n)
                    dn_done_b[fb] = sWR2.n

            @block.scalar
            def _(s):
                s.wait_ge(cZP2[0].h, 16 * 32)
                s.wait_ge(cZP2[1].h, 16 * 32)
                for fb in range(8):
                    for t in range(8):
                        idx = 8 * fb + t
                        if idx >= 2:
                            s.wait_ge(sZE2.h, idx - 1)
                        cZL2[idx % 2].inc(s.dma_start(
                            out=ZPT[:, idx % 2],
                            in_=SCR_ZP[t][:, fb * 512:(fb + 1) * 512]), 16)

            @block.sync
            def _(q):
                for fb in range(8):
                    for t in range(8):
                        idx = 8 * fb + t
                        q.wait_ge(sZE2.h, idx + 1)
                        cZ2[idx % 2].inc(q.dma_start(
                            out=SCR_Z[t][:, fb * 512:(fb + 1) * 512],
                            in_=ZSTG[:, idx % 2]), 16)

            @block.vector
            def _(v):
                v.wait_ge(cZP2[0].h, 16 * 32)
                v.wait_ge(cZP2[1].h, 16 * 32)
                for fb in range(8):
                    v.wait_ge(sWR2.h, dn_done_b[fb])
                    for t in range(8):
                        idx = 8 * fb + t
                        v.wait_ge(cZL2[idx % 2].h, 16 * (idx // 2 + 1))
                        if idx >= 2:
                            v.wait_ge(cZ2[idx % 2].h, 16 * (idx // 2))
                        sZE2.inc(v.tensor_tensor(
                            out=ZSTG[:, idx % 2],
                            in0=ZPSB[:, t * 512:(t + 1) * 512],
                            in1=ZPT[:, idx % 2], op=OP.add))


        if upto >= 6:
            p6bps.close()
            p56.close()
            p45.close()
            p7 = ExitStack()
            GROW = p7.enter_context(
                nc.sbuf_tensor("GROW", [1, D], F32, side="right"))
            ONESR = p7.enter_context(
                nc.sbuf_tensor("ONESR", [1, 128], F32, side="right"))
            SQJ7 = p7.enter_context(
                nc.sbuf_tensor("SQJ7", [128, D], F32, side="right"))
            p7ps = ExitStack()
            BCP = p7ps.enter_context(nc.psum_tensor("BCP", [128, 512], F32))

            ZT = [AV32[:, k * 4096:(k + 1) * 4096] for k in range(2)]
            YT = [AV32[:, (2 + k) * 4096:(3 + k) * 4096] for k in range(2)]
            GAMB = BV32[:, 0:4096]
            BETB = BV32[:, 4096:8192]
            WJ = [BV32[:, (2 + k) * 4096:(3 + k) * 4096] for k in range(2)]

            @block.sync
            def _(q):
                q.wait_ge(sWR2.h, dn_done_b[7])
                q.wait_ge(sUP.h, 128)
                cG.inc(q.dma_start(out=GROW[:], in_=GAMMA[:]), 16)
                q.wait_ge(sGBm.h, 8)
                cG.inc(q.dma_start(out=GROW[:], in_=BETA[:]), 16)

            @block.vector
            def _(v):
                v.wait_ge(sUP.h, 128)
                v.memset(ONESR[:], 1.0)
                sGB.inc(v.memset(sml(38), 0.0))

            @block.scalar
            def _(s):
                for k in range(16):
                    s.wait_ge(sGBm.h, k + 1)
                    dst = GAMB if k < 8 else BETB
                    j8 = k % 8
                    sGBe.inc(s.activation(out=dst[:, j8 * 512:(j8 + 1) * 512],
                                          in_=BCP[:], func=AF.Copy))

            @block.tensor
            def _(te):
                te.wait_ge(sGB.h, 1)
                te.wait_ge(sZE2.h, 64)
                for k in range(16):
                    te.wait_ge(cG.h, 16 if k < 8 else 32)
                    if k > 0:
                        te.wait_ge(sGBe.h, k)
                    j8 = k % 8
                    sGBm.inc(te.matmul(BCP[:], ONESR[:],
                                       GROW[0:1, j8 * 512:(j8 + 1) * 512],
                                       start=True, stop=True))

            @block.sync
            def _(q):
                q.wait_ge(cZ2[0].h, 16 * 32)
                q.wait_ge(cZ2[1].h, 16 * 32)
                for sl in range(NS):
                    if sl >= 2:
                        q.wait_ge(sTG.h, sl - 1)
                    cA2[sl % 2].inc(q.dma_start(out=ZT[sl % 2],
                                                in_=SCR_Z[sl]), 16)
                    if sl >= 1:
                        q.wait_ge(sO.h, sl)
                        cO2[(sl - 1) % 2].inc(
                            q.dma_start(out=OUT[sl - 1],
                                        in_=YT[(sl - 1) % 2]), 16)
                q.wait_ge(sO.h, NS)
                cO2[(NS - 1) % 2].inc(
                    q.dma_start(out=OUT[NS - 1], in_=YT[(NS - 1) % 2]), 16)
                q.wait_ge(cO2[0].h, 16 * (NS // 2))
                q.wait_ge(cO2[1].h, 16 * (NS // 2))

            @block.scalar
            def _(s):
                for sl in range(NS):
                    par = sl % 2
                    kz, kw = 32 + 2 * sl, 33 + 2 * sl
                    if sl >= 2:
                        s.wait_ge(cO2[par].h, 16 * (sl // 2))
                    cX2[par].inc(s.dma_start(out=YT[par], in_=SCR_Y[sl]), 16)
                    s.wait_ge(cA2[par].h, 16 * (NS // 2 + sl // 2 + 1))
                    ln_act(s, kz, ZT[par], SQJ7[:])
                    s.wait_ge(sWW.h, sl + 1)
                    ln_act(s, kw, WJ[par], SQJ7[:])
                    sUN.inc(s.activation(out=WJ[par], in_=WJ[par],
                                         func=AF.Identity,
                                         scale=sml(12 + kw % 2),
                                         bias=sml(14 + kw % 2)))

            @block.vector
            def _(v):
                v.wait_ge(sGBe.h, 16)
                for sl in range(NS):
                    par = sl % 2
                    kz, kw = 32 + 2 * sl, 33 + 2 * sl
                    v.wait_ge(cA2[par].h, 16 * (NS // 2 + sl // 2 + 1))
                    ln_dve_sum(v, kz, ZT[par])
                    ln_dve_var(v, kz)
                    v.wait_ge(sRSTD.h, kz + 1)
                    sIA.inc(v.tensor_scalar(out=ZT[par], in0=ZT[par],
                                            scalar1=sml(12 + kz % 2),
                                            scalar2=sml(14 + kz % 2),
                                            op0=OP.mult, op1=OP.add))
                    if sl >= 2:
                        v.wait_ge(sUN.h, sl - 1)
                    sTG.inc(v.tensor_tensor(out=WJ[par], in0=ZT[par],
                                            in1=GAMB, op=OP.mult))
                    sWW.inc(v.tensor_tensor(out=WJ[par], in0=WJ[par],
                                            in1=BETB, op=OP.add))
                    ln_dve_sum(v, kw, WJ[par])
                    ln_dve_var(v, kw)
                    v.wait_ge(sUN.h, sl + 1)
                    v.wait_ge(cX2[par].h, 16 * (NS // 2 + sl // 2 + 1))
                    sO.inc(v.tensor_tensor(out=YT[par], in0=YT[par],
                                           in1=WJ[par], op=OP.add))



        if qg_sched:
            @block.gpsimd
            def _(q):
                for kind, ent in qg_sched:
                    if kind == "wsb":
                        i = ent
                        srcap, kdim = wsb_sched[i]
                        if i >= 2:
                            q.wait_ge(*wsb_release[i - 2])
                        cB2[i % 2].inc(
                            q.dma_start(out=wsb_view(i, kdim), in_=srcap), 16)
                    else:
                        dst_ap, src_ap, sem, waits = ent
                        for wh, wn in waits:
                            q.wait_ge(wh, wn)
                        sem.inc(q.dma_start(out=dst_ap, in_=src_ap), 16)

        @block.sync
        def _(q):
            for sm in ([cXL, cSC, cY, cZP, cZ, cO, cT, cG] + cXL2 + cY2
                       + cZ2 + cO2 + cA2 + cS4):
                q.wait_ge(sm.h, sm.n)

        @block.gpsimd
        def _(q):
            for sm in [cW, cEXA] + cQ2 + cB2:
                q.wait_ge(sm.h, sm.n)

        @block.scalar
        def _(s):
            for sm in [sP4X] + cX2 + cZL2 + cZP2:
                s.wait_ge(sm.h, sm.n)

    return nc, taps_out


def _host_inputs(core, shared, x_full):
    b, r = core // 2, core % 2
    xt = np.asarray(x_full)[b].reshape(NT, 128, D)
    order = []
    for j in range(NS):
        order.append(2 * j + r)
        order.append(2 * j + (1 - r))
    XBc = np.ascontiguousarray(xt[order])
    kpos = np.empty(S2, np.float32)
    for p, gt in enumerate(order):
        kpos[p * 128:(p + 1) * 128] = np.arange(gt * 128, (gt + 1) * 128,
                                                dtype=np.float32)
    KL2c = np.ascontiguousarray(np.stack([kpos,
                                          np.full(S2, -1.0, np.float32)]))
    qpos = np.empty(1024, np.float32)
    for s in range(NS):
        gt = 2 * s + r
        qpos[s * 128:(s + 1) * 128] = np.arange(gt * 128, (gt + 1) * 128,
                                                dtype=np.float32)
    Q2c = np.ascontiguousarray(np.stack([np.ones(1024, np.float32), qpos]))
    tri = np.where(np.arange(128)[None, :] > np.arange(128)[:, None],
                   np.float32(NEGBIG), np.float32(0.0)).astype(np.float32)
    mrol = (np.full((128, 128), NEGBIG, np.float32) if r == 0
            else np.zeros((128, 128), np.float32))
    d = {"XB": XBc, "KL2": KL2c, "Q2": Q2c,
         "MTRI": np.ascontiguousarray(tri), "MROL": np.ascontiguousarray(mrol)}
    d.update(shared)
    return d


def _shared_weights(i):
    def tile4(w, fblk):
        w = np.asarray(w)
        K, F = w.shape
        return np.ascontiguousarray(
            w.reshape(K // 128, 128, F // fblk, fblk).transpose(2, 0, 1, 3))
    return {
        "WQ4": tile4(i["L0_attn_Wq"], 128),
        "WKV4": tile4(i["L0_attn_Wkv"], 128),
        "WO4": tile4(i["L0_attn_Wo"], 512),
        "UP4": tile4(i["L0_ffn_up_w"], 128),
        "DN4": tile4(i["L0_ffn_down_w"], 512),
        "F14": tile4(i["L0_ffn_fc1_w"], 128),
        "EX4": tile4(i["L0_ffn_expert_w"], 512),
        "F24": tile4(i["L0_ffn_fc2_w"], 512),
        "GTH": np.ascontiguousarray(
            np.asarray(i["L0_ffn_gate_w"]).reshape(KC, 128, NE)
            .transpose(1, 0, 2).reshape(128, KC * NE)),
        "GAMMA": np.ascontiguousarray(
            np.asarray(i["L0_ffn_ln_gamma"], np.float32)[None, :]),
        "BETA": np.ascontiguousarray(
            np.asarray(i["L0_ffn_ln_beta"], np.float32)[None, :]),
        "IDBH": np.eye(128).astype(ml_dtypes.bfloat16),
    }


def kernel(**inputs):
    shared = _shared_weights(inputs)
    x_full = np.asarray(inputs["L0_attn_hidden_states"], np.float32)
    in_maps = [_host_inputs(c, shared, x_full) for c in range(8)]
    nc, taps_out = build_nc()
    res = run_bass_kernel_spmd(nc, in_maps, core_ids=list(range(8)))
    out = np.empty_like(x_full)
    for c in range(8):
        b, r = c // 2, c % 2
        o = np.asarray(res.results[c]["OUT"])
        for s in range(NS):
            gt = 2 * s + r
            out[b, gt * 128:(gt + 1) * 128, :] = o[s]
    kernel.last_results = res.results if TAPS else None
    return out


# version 02
# speedup vs baseline: 1.0587x; 1.0587x over previous
import numpy as np
import ml_dtypes
import concourse.bass as bass
import concourse.mybir as mybir
from concourse.alu_op_type import AluOpType
from concourse.bass_utils import run_bass_kernel_spmd
from contextlib import ExitStack

F32 = mybir.dt.float32
BF16 = mybir.dt.bfloat16
FP16 = mybir.dt.float16
AF = mybir.ActivationFunctionType
AX = mybir.AxisListType
OP = AluOpType

D = 4096
KC = 32
S2 = 2048
NT = 16
NS = 8
HN = 32
IB = 64
LC = 16
NE = 8
EPS = 1e-5
NEGBIG = -1.0e9
SCALE = 128.0 ** -0.5

UPTO = "P7"
TAPS = ()
SKIP_LN2T = False
P45_PARTS = "full"
PHASES = ["P1", "P2", "P3", "P4", "P45", "P56", "P7"]


def _slope(h):
    return 2.0 ** (-0.25 * (h + 1))


class Sem:
    def __init__(self, h):
        self.h = h
        self.n = 0

    def inc(self, instr, k=1):
        instr.then_inc(self.h, k)
        self.n += k
        return instr


def build_nc():
    upto = PHASES.index(UPTO)
    taps = set(TAPS)
    nc = bass.Bass("TRN2", target_bir_lowering=False)

    XB = nc.dram_tensor("XB", [NT, 128, D], F32, kind="ExternalInput")
    WQ4 = nc.dram_tensor("WQ4", [HN, KC, 128, 128], F32, kind="ExternalInput")
    WKV4 = nc.dram_tensor("WKV4", [2, KC, 128, 128], F32, kind="ExternalInput")
    WO4 = nc.dram_tensor("WO4", [8, KC, 128, 512], F32, kind="ExternalInput")
    UP4 = nc.dram_tensor("UP4", [IB, KC, 128, 128], F32, kind="ExternalInput")
    DN4 = nc.dram_tensor("DN4", [8, IB, 128, 512], F32, kind="ExternalInput")
    F14 = nc.dram_tensor("F14", [LC, KC, 128, 128], F32, kind="ExternalInput")
    EX4 = nc.dram_tensor("EX4", [4, LC, 128, 512], F32, kind="ExternalInput")
    F24 = nc.dram_tensor("F24", [8, LC, 128, 512], F32, kind="ExternalInput")
    GTH = nc.dram_tensor("GTH", [128, KC * NE], F32, kind="ExternalInput")
    GAMMA = nc.dram_tensor("GAMMA", [1, D], F32, kind="ExternalInput")
    BETA = nc.dram_tensor("BETA", [1, D], F32, kind="ExternalInput")
    IDBH = nc.dram_tensor("IDBH", [128, 128], BF16, kind="ExternalInput")
    MTRI = nc.dram_tensor("MTRI", [128, 128], F32, kind="ExternalInput")
    MROL = nc.dram_tensor("MROL", [128, 128], F32, kind="ExternalInput")
    KL2 = nc.dram_tensor("KL2", [2, S2], F32, kind="ExternalInput")
    Q2 = nc.dram_tensor("Q2", [2, 1024], F32, kind="ExternalInput")

    OUT = nc.dram_tensor("OUT", [NS, 128, D], F32, kind="ExternalOutput")

    def scratch(name, tapname):
        kind = "ExternalOutput" if tapname in taps else "Internal"
        return nc.dram_tensor(name, [NS, 128, D], F32, kind=kind)

    SCR_ATT = scratch("SCR_ATT", "attnout")
    SCR_Y = scratch("SCR_Y", "y")
    SCR_ZP = scratch("SCR_ZP", "zp")
    SCR_Z = scratch("SCR_Z", "z")

    taps_out = {}
    def tap(name, shape, dtype):
        if name in taps:
            taps_out[name] = nc.dram_tensor("tap_" + name, shape, dtype,
                                            kind="ExternalOutput")
            return taps_out[name]
        return None

    t_ln1fm = tap("ln1fm", [128, KC * 1024], BF16)
    t_kf = tap("kf", [128, S2], BF16)
    t_vt = tap("vt", [128, NT * 128], BF16)
    t_q0 = tap("q0", [128, 1024], BF16)
    t_wq0 = tap("wq0", [128, KC * 128], BF16)
    t_afm = tap("afm", [128, HN * 1024], BF16)
    t_ln2fm = tap("ln2fm", [128, KC * 1024], BF16)
    t_route = tap("route", [128, NS], F32)
    t_xlat = tap("xlat", [128, LC * 1024], BF16)
    t_eos = tap("eos", [128, LC * 1024], BF16)
    t_h1 = tap("h1", [128, 32 * 1024], BF16)

    with ExitStack() as st:
        SEM = lambda n: Sem(st.enter_context(nc.semaphore(n)))

        def SBL(n, s, d):
            return st.enter_context(nc.sbuf_tensor(n, s, d, side="left"))
        WSB = SBL("WSB", [128, 2, 4096], BF16)
        IDB = SBL("IDB", [128, 128], BF16)
        MA = SBL("MA", [128, 128], F32)
        MB = SBL("MB", [128, 128], F32)
        GTB = SBL("GTB", [128, KC * NE], BF16)
        A = SBL("A", [128, KC * 1024], BF16)
        B = SBL("B", [128, KC * 1024], BF16)
        SML = SBL("SML", [128, 40], F32)

        def sml(i):
            return SML[:, i:i + 1]

        AV32 = A[:].bitcast(F32)
        BV32 = B[:].bitcast(F32)

        sems = {}
        for name in ("cW sSQ1 sVAR sRSTD sLN sTP sLNF sKV sKVE sVT sVTE sP1 sQP sQF sSCB sEXP sPT sPRT sAV sAVE sATP sAFM sWOE sT1 sY sLN2 sTP2 sLNF2 sP4 sGL sRT sF1 sF1E sEX sEXE sTPE sEOS sUP sREL sZE sZE2 sWR2 cEXA sIA sTG sWW sUN sO cT sGB cG sGBm sGBe sQB sRCP sGLE sM1 sG2 sM2 sM1B sEQ sG2B sM2R cXLa cXLb cQa cQb cBa cBb cAa cAb cXa cXb cYa cYb cZPa cZPb cZLa cZLb cZa cZb cOa cOb cS0 cS1 cS2 cS3 sMK sNBF sKMa sKMb sKMc sKMd sKMe sKMf sONE sQSQ sQN sQRW sQNT").split():
            sems[name] = SEM(name)
        g = sems
        cW = g["cW"]
        sSQ1, sVAR, sRSTD, sLN = g["sSQ1"], g["sVAR"], g["sRSTD"], g["sLN"]
        sTP, sLNF, sKV, sKVE = g["sTP"], g["sLNF"], g["sKV"], g["sKVE"]
        sVT, sVTE, sP1 = g["sVT"], g["sVTE"], g["sP1"]
        sQP, sQF, sSCB = g["sQP"], g["sQF"], g["sSCB"]
        sEXP, sPT, sPRT, sAV = g["sEXP"], g["sPT"], g["sPRT"], g["sAV"]
        sAVE, sATP, sAFM = g["sAVE"], g["sATP"], g["sAFM"]
        sWOE = g["sWOE"]
        sT1, sY = g["sT1"], g["sY"]
        sLN2, sTP2, sLNF2, sP4 = g["sLN2"], g["sTP2"], g["sLNF2"], g["sP4"]
        sGL, sRT, sF1, sF1E = g["sGL"], g["sRT"], g["sF1"], g["sF1E"]
        sEX, sEXE, sTPE, sEOS = g["sEX"], g["sEXE"], g["sTPE"], g["sEOS"]
        sUP, sREL, sZE = g["sUP"], g["sREL"], g["sZE"]
        sZE2 = g["sZE2"]
        sWR2, cEXA = g["sWR2"], g["cEXA"]
        sIA, sTG, sWW, sUN, sO, cT = (g["sIA"], g["sTG"], g["sWW"],
                                      g["sUN"], g["sO"], g["cT"])
        sGB = g["sGB"]
        cG, sGBm, sGBe, sQB = g["cG"], g["sGBm"], g["sGBe"], g["sQB"]
        sRCP, sGLE, sM1, sG2, sM2 = (g["sRCP"], g["sGLE"], g["sM1"],
                                     g["sG2"], g["sM2"])
        sM1B, sEQ, sG2B, sM2R = (g["sM1B"], g["sEQ"], g["sG2B"],
                                 g["sM2R"])
        cXL2 = [g["cXLa"], g["cXLb"]]
        cQ2 = [g["cQa"], g["cQb"]]
        cB2 = [g["cBa"], g["cBb"]]
        cA2 = [g["cAa"], g["cAb"]]
        cX2 = [g["cXa"], g["cXb"]]
        cY2 = [g["cYa"], g["cYb"]]
        cZP2 = [g["cZPa"], g["cZPb"]]
        cZL2 = [g["cZLa"], g["cZLb"]]
        cZ2 = [g["cZa"], g["cZb"]]
        cO2 = [g["cOa"], g["cOb"]]
        cS4 = [g["cS0"], g["cS1"], g["cS2"], g["cS3"]]
        sMK, sNBF = g["sMK"], g["sNBF"]
        sKMa, sKMb, sKMc, sKMd, sKMe, sKMf, sONE = (
            g["sKMa"], g["sKMb"], g["sKMc"], g["sKMd"], g["sKMe"], g["sKMf"],
            g["sONE"])
        sQSQ, sQN, sQRW, sQNT = g["sQSQ"], g["sQN"], g["sQRW"], g["sQNT"]

        def ln_dve_sum(v, k, srcap):
            p = k % 2
            sVAR.inc(v.tensor_reduce(out=sml(0 + p), in_=srcap, axis=AX.X,
                                     op=OP.add))

        def ln_dve_var(v, k):
            p = k % 2
            v.wait_ge(sSQ1.h, k + 1)
            sVAR.inc(v.tensor_scalar(out=sml(8 + p), in0=sml(2 + p),
                                     scalar1=1.0 / D, scalar2=sml(6 + p),
                                     op0=OP.mult, op1=OP.subtract))

        def ln_act(s, k, srcap, junk):
            p = k % 2
            s.activation(out=junk, in_=srcap, func=AF.Square,
                         accum_out=sml(2 + p))
            s.wait_ge(sVAR.h, 2 * k + 1)
            s.activation(out=sml(4 + p), in_=sml(0 + p), func=AF.Identity,
                         scale=-1.0 / D)
            sSQ1.inc(s.activation(out=sml(6 + p), in_=sml(4 + p),
                                  func=AF.Square))
            s.wait_ge(sVAR.h, 2 * k + 2)
            s.activation(out=sml(10 + p), in_=sml(8 + p), func=AF.Ln,
                         bias=sml(39))
            s.activation(out=sml(12 + p), in_=sml(10 + p), func=AF.Exp,
                         scale=-0.5)
            sRSTD.inc(s.activation(out=sml(14 + p), in_=sml(4 + p),
                                   func=AF.Identity, scale=sml(12 + p)))

        wsb_sched = []
        wsb_release = {}
        qg_sched = []

        def wsb_push(src_ap, kdim):
            wsb_sched.append((src_ap, kdim))
            qg_sched.append(("wsb", len(wsb_sched) - 1))
            return len(wsb_sched) - 1

        def qg_raw_push(dst_ap, src_ap, sem, waits):
            qg_sched.append(("raw", (dst_ap, src_ap, sem, list(waits))))

        def wsb_view(i, kdim):
            return WSB[:, i % 2].rearrange("p (k c) -> p k c", k=kdim)

        block = st.enter_context(nc.Block())

        _scope_state = {}
        def scope_next(nm):
            if "cur" in _scope_state:
                nc.leave_named_scope(_scope_state["cur"][0],
                                     _scope_state["cur"][1], True)
            sid, _ = nc.enter_named_scope(nm, True)
            _scope_state["cur"] = (nm, sid)


        p1 = ExitStack()
        def SBR1(n, s, d):
            return p1.enter_context(nc.sbuf_tensor(n, s, d, side="right"))
        KF = SBR1("KF", [128, S2], BF16)
        VT = SBR1("VT", [128, NT, 128], BF16)
        QFB = SBR1("QFB", [128, 2, 1024], BF16)
        KL2F = SBR1("KL2F", [2, S2], FP16)
        Q2S = SBR1("Q2S", [2, 1024], F32)
        p1tmp = ExitStack()
        def SBR1T(n, s, d):
            return p1tmp.enter_context(nc.sbuf_tensor(n, s, d, side="right"))
        WKVB = SBR1T("WKVB", [128, 2, KC, 128], BF16)
        LNF = SBR1T("LNF", [128, D], BF16)
        VSB = SBR1T("VSB", [128, 128], BF16)

        XT = [BV32[:, k * 4096:(k + 1) * 4096] for k in range(2)]
        LNT = [B[:, 16384 + k * 4096: 16384 + (k + 1) * 4096] for k in range(2)]
        SQJ = BV32[:, 12288:16384]

        p1ps = ExitStack()
        TPP = [p1ps.enter_context(nc.psum_tensor(f"TP{k}", [128, 1024], BF16))
               for k in range(2)]
        KVK = p1ps.enter_context(nc.psum_tensor("KVK", [128, 128], F32))
        KVV = p1ps.enter_context(nc.psum_tensor("KVV", [128, 128], F32))
        VTP = p1ps.enter_context(nc.psum_tensor("VTP", [128, 128], BF16))

        @block.gpsimd
        def _(q):
            cW.inc(q.dma_start(out=MA[:], in_=MTRI[:]), 16)
            cW.inc(q.dma_start(out=MB[:], in_=MROL[:]), 16)
            cW.inc(q.dma_start(out=KL2F[:], in_=KL2[:]), 16)
            cW.inc(q.dma_start(out=Q2S[:], in_=Q2[:]), 16)
            cW.inc(q.dma_start(out=GTB[:], in_=GTH[:]), 16)
            for kv in range(2):
                cW.inc(q.dma_start(out=WKVB[:, kv],
                                   in_=WKV4[kv].rearrange("k p j -> p k j")), 16)
            cW.inc(q.dma_start(out=IDB[:], in_=IDBH[:]), 16)
        CW_CONST = cW.n

        scope_next("P1")
        @block.sync
        def _(q):
            for t in range(NT):
                if t >= 2:
                    q.wait_ge(sLN.h, t - 1)
                    q.wait_ge(sSQ1.h, t - 1)
                cXL2[t % 2].inc(q.dma_start(out=XT[t % 2], in_=XB[t]), 16)

        @block.scalar
        def _(s):
            for t in range(NT):
                s.wait_ge(cXL2[t % 2].h, 16 * (t // 2 + 1))
                ln_act(s, t, XT[t % 2], SQJ)

        @block.vector
        def _(v):
            v.memset(sml(39), EPS)
            for t in range(NT):
                par = t % 2
                v.wait_ge(cXL2[par].h, 16 * (t // 2 + 1))
                ln_dve_sum(v, t, XT[par])
                ln_dve_var(v, t)
                v.wait_ge(sRSTD.h, t + 1)
                sLN.inc(v.tensor_scalar(out=LNT[par], in0=XT[par],
                                        scalar1=sml(12 + par),
                                        scalar2=sml(14 + par),
                                        op0=OP.mult, op1=OP.add))
                for gi in range(4):
                    v.wait_ge(sTP.h, 4 * t + gi + 1)
                    if t % 2 == 0:
                        so = t // 2
                        dstap = A[:].rearrange("p (k c) -> p k c", k=KC)[
                            :, gi * 8:(gi + 1) * 8, so * 128:so * 128 + 128]
                    else:
                        dstap = LNF[:].rearrange("p (k c) -> p k c", k=KC)[
                            :, gi * 8:(gi + 1) * 8, 0:128]
                    srcap = TPP[gi % 2][:].rearrange("p (k c) -> p k c", k=8)
                    sLNF.inc(v.tensor_copy(dstap, srcap))
                v.wait_ge(sKV.h, 2 * t + 1)
                sKVE.inc(v.tensor_copy(KF[:, t * 128:(t + 1) * 128], KVK[:]))
                v.wait_ge(sKV.h, 2 * t + 2)
                sKVE.inc(v.tensor_copy(VSB[:], KVV[:]))
                v.wait_ge(sVT.h, t + 1)
                sVTE.inc(v.tensor_copy(VT[:, t, :], VTP[:]))
            v.wait_ge(sLNF.h, 4 * NT)
            sP1.inc(v.memset(sml(38), 0.0))

        @block.tensor
        def _(te):
            te.wait_ge(cW.h, CW_CONST)
            for t in range(NT):
                par = t % 2
                te.wait_ge(sLN.h, t + 1)
                for gi in range(4):
                    gidx = 4 * t + gi
                    if gidx >= 2:
                        te.wait_ge(sLNF.h, gidx - 1)
                    for k8 in range(8):
                        kc = gi * 8 + k8
                        tr = te.transpose(TPP[gi % 2][:, k8 * 128:(k8 + 1) * 128],
                                          LNT[par][:, kc * 128:(kc + 1) * 128],
                                          IDB[:])
                    sTP.inc(tr)
                te.wait_ge(sLNF.h, 4 * t + 4)
                if t >= 1:
                    te.wait_ge(sKVE.h, 2 * t)
                if t % 2 == 0:
                    so = t // 2
                    rview = A[:].rearrange("p (k c) -> p k c", k=KC)[
                        :, :, so * 128:so * 128 + 128]
                else:
                    rview = LNF[:].rearrange("p (k c) -> p k c", k=KC)
                for kv, pdst in ((0, KVK), (1, KVV)):
                    for kc in range(KC):
                        mm = te.matmul(pdst[:], WKVB[:, kv, kc], rview[:, kc],
                                       start=(kc == 0), stop=(kc == KC - 1))
                    sKV.inc(mm)
                te.wait_ge(sKVE.h, 2 * t + 2)
                if t >= 1:
                    te.wait_ge(sVTE.h, t)
                sVT.inc(te.transpose(VTP[:], VSB[:], IDB[:]))

        if t_ln1fm is not None:
            @block.sync
            def _(q):
                q.wait_ge(sP1.h, 1)
                cT.inc(q.dma_start(out=t_ln1fm[:], in_=A[:]), 16)
        if t_kf is not None:
            @block.sync
            def _(q):
                q.wait_ge(sP1.h, 1)
                cT.inc(q.dma_start(out=t_kf[:], in_=KF[:]), 16)
        if t_vt is not None:
            @block.sync
            def _(q):
                q.wait_ge(sP1.h, 1)
                cT.inc(q.dma_start(out=t_vt[:],
                                   in_=VT[:].rearrange("p t c -> p (t c)")), 16)

        if upto >= 1:
            scope_next("P2")
            p1tmp.close()
            p2 = ExitStack()
            def SBR2(n, s, d):
                return p2.enter_context(nc.sbuf_tensor(n, s, d, side="right"))
            PR = SBR2("PR", [128, S2], BF16)
            PRT = SBR2("PRT", [128, S2], BF16)
            WQB = SBR2("WQB", [128, 2, KC, 128], BF16)
            QB2 = SBR2("QB2", [2, 2, 1024], FP16)
            ATTB = SBR2("ATTB", [128, 2, 128], BF16)
            NBF = SBR2("NBF", [128, 2, 8], F32)
            KR = SBR2("KR", [1, 512], F32)
            KMPX = SBR2("KMPX", [1, 8], F32)
            KMB = SBR2("KMB", [128, 1], F32)
            ONESC = SBR2("ONESC", [128, 1], F32)
            ONES1R = SBR2("ONES1R", [1, 128], F32)
            ONES33 = SBR2("ONES33", [33, 1], BF16)
            ON_A = SBR2("ON_A", [128, 33], BF16)
            ON_B = SBR2("ON_B", [128, 33], BF16)
            QSQF = SBR2("QSQF", [128, 1024], BF16)
            QROW = SBR2("QROW", [33, 512], BF16)
            KSQ = BV32[:, 0:2048]

            p1ps.close()
            p2ps = ExitStack()
            SC = p2ps.enter_context(nc.psum_tensor("SC", [128, 2048], F32))
            PTP2 = [p2ps.enter_context(nc.psum_tensor(f"PTP{k}", [128, 1024],
                                                      BF16)) for k in range(2)]
            AVP = p2ps.enter_context(nc.psum_tensor("AVP", [128, 128], F32))
            QPS = p2ps.enter_context(nc.psum_tensor("QPS", [128, 512], F32))
            ATP = PTP2[0][:, 0:128]

            def nblk(sl):
                return ((2 * sl + 2) * 128 + 511) // 512

            def bw(sl, j):
                return min(512, (2 * sl + 2) * 128 - j * 512)

            NBLK_H = sum(nblk(x) for x in range(NS))
            NPTG_S = [(2 * x + 2 + 7) // 8 for x in range(NS)]
            NPTG_H = sum(NPTG_S)

            @block.gpsimd
            def _(q):
                for h in range(HN):
                    if h == 0:
                        q.wait_ge(sP1.h, 1)
                    if h >= 2:
                        q.wait_ge(sQP.h, 2 * (h - 1))
                        if h == 2 and (t_q0 is not None or t_wq0 is not None):
                            q.wait_ge(cT.h, 16 * ((t_q0 is not None)
                                                  + (t_wq0 is not None)))
                    cQ2[h % 2].inc(
                        q.dma_start(out=WQB[:, h % 2],
                                    in_=WQ4[h].rearrange("k p j -> p k j")),
                        16)

            @block.scalar
            def _(s):
                s.wait_ge(cW.h, CW_CONST)
                s.wait_ge(sP1.h, 1)
                sKMa.inc(s.activation(out=KSQ, in_=KF[:], func=AF.Square))
                for j in range(4):
                    s.wait_ge(sKMb.h, j + 1)
                    sKMc.inc(s.activation(out=KR[:], in_=QPS[0:1, 0:512],
                                          func=AF.Copy))
                s.wait_ge(sKMd.h, 4)
                sKMe.inc(s.activation(out=KMPX[0:1, 4:8],
                                      in_=KMPX[0:1, 0:4], func=AF.Identity))
                s.wait_ge(sKMd.h, 5)
                s.activation(out=KMPX[0:1, 1:2], in_=KMPX[0:1, 0:1],
                             func=AF.Ln)
                s.activation(out=KMPX[0:1, 1:2], in_=KMPX[0:1, 1:2],
                             func=AF.Exp, scale=0.5)
                sKMe.inc(s.activation(out=KMPX[0:1, 1:2], in_=KMPX[0:1, 1:2],
                                      func=AF.Identity, scale=-1.0))
                s.wait_ge(sKMb.h, 5)
                sKMf.inc(s.activation(out=KMB[:], in_=QPS[:, 0:1],
                                      func=AF.Copy))
                ptg_before = 0
                for h in range(HN):
                    if h >= 2:
                        s.wait_ge(sSCB.h, h * NBLK_H - NBLK_H + 1)
                    sQB.inc(s.activation(out=QB2[:, h % 2], in_=Q2S[:],
                                         func=AF.Identity, scale=_slope(h)))
                    s.wait_ge(sQF.h, 2 * h + 2)
                    sQSQ.inc(s.activation(out=QSQF[:], in_=QFB[:, h % 2],
                                          func=AF.Square))
                    s.wait_ge(sQN.h, h + 1)
                    sQRW.inc(s.activation(out=QROW[:], in_=QPS[0:33, 0:512],
                                          func=AF.Copy))
                    s.wait_ge(sQNT.h, h + 1)
                    s.activation(out=NBF[:, h % 2], in_=QPS[:, 8:16],
                                 func=AF.Copy)
                    s.activation(out=NBF[:, h % 2], in_=NBF[:, h % 2],
                                 func=AF.Ln, bias=sml(39))
                    s.activation(out=NBF[:, h % 2], in_=NBF[:, h % 2],
                                 func=AF.Exp, scale=0.5)
                    sNBF.inc(s.activation(out=NBF[:, h % 2], in_=NBF[:, h % 2],
                                          func=AF.Identity, scale=KMB[:]))
                    for sl in range(NS):
                        it = h * NS + sl
                        par = it % 2
                        ka = (2 * sl + 2) * 128
                        s.wait_ge(sMK.h, it + 1)
                        s.wait_ge(sNBF.h, h + 1)
                        if it > 0:
                            s.wait_ge(sPT.h, ptg_before)
                        sEXP.inc(s.activation(out=PR[:, 0:ka], in_=SC[:, 0:ka],
                                              func=AF.Exp,
                                              bias=NBF[:, h % 2, sl:sl + 1],
                                              accum_out=sml(18 + par)))
                        ptg_before += NPTG_S[sl]
                        base_g = h * NPTG_H + sum(NPTG_S[:sl])
                        for gi in range(NPTG_S[sl]):
                            n128 = min(8, 2 * sl + 2 - gi * 8)
                            s.wait_ge(sPT.h, base_g + gi + 1)
                            sPRT.inc(s.activation(
                                out=PRT[:, gi * 1024:gi * 1024 + n128 * 128],
                                in_=PTP2[gi % 2][:, 0:n128 * 128],
                                func=AF.Copy))
                        s.wait_ge(sAV.h, it + 1)
                        s.wait_ge(sRCP.h, it + 1)
                        if it >= 2:
                            s.wait_ge(sATP.h, it - 1)
                        sAVE.inc(s.activation(out=ATTB[:, par], in_=AVP[:],
                                              func=AF.Identity,
                                              scale=sml(20 + par)))
                        s.wait_ge(sATP.h, it + 1)
                        sAFM.inc(s.activation(
                            out=B[:, h * 1024 + sl * 128:
                                 h * 1024 + (sl + 1) * 128],
                            in_=ATP, func=AF.Copy))

            @block.vector
            def _(v):
                v.memset(ONESC[:], 1.0)
                v.memset(ONES1R[:], 1.0)
                v.memset(ONES33[:], 1.0)
                v.memset(ON_A[:], 0.0)
                v.memset(ON_A[:, 0:1], 1.0)
                v.memset(ON_B[:], 0.0)
                v.memset(ON_B[:, 32:33], 1.0)
                sONE.inc(v.memset(KMPX[:], 0.0))
                for j in range(4):
                    v.wait_ge(sKMc.h, j + 1)
                    sKMd.inc(v.tensor_reduce(out=KMPX[0:1, j:j + 1],
                                             in_=KR[:], axis=AX.X, op=OP.max))
                v.wait_ge(sKMe.h, 1)
                sKMd.inc(v.tensor_reduce(out=KMPX[0:1, 0:1],
                                         in_=KMPX[0:1, 4:8], axis=AX.X,
                                         op=OP.max))
                for h in range(HN):
                    for half in range(2):
                        v.wait_ge(sQP.h, 2 * h + half + 1)
                        sQF.inc(v.tensor_scalar(
                            out=QFB[:, h % 2, half * 512:(half + 1) * 512],
                            in0=QPS[:], scalar1=SCALE, scalar2=None,
                            op0=OP.mult, op1=OP.bypass))
                    for sl in range(NS):
                        it = h * NS + sl
                        par = it % 2
                        nb = nblk(sl)
                        base_b = h * NBLK_H + sum(nblk(x) for x in range(sl))
                        ka = (2 * sl + 2) * 128
                        v.wait_ge(sSCB.h, base_b + nb)
                        v.tensor_tensor(out=SC[:, ka - 256:ka - 128],
                                        in0=SC[:, ka - 256:ka - 128],
                                        in1=MA[:], op=OP.add)
                        sMK.inc(v.tensor_tensor(out=SC[:, ka - 128:ka],
                                                in0=SC[:, ka - 128:ka],
                                                in1=MB[:], op=OP.add))
                        v.wait_ge(sEXP.h, it + 1)
                        sRCP.inc(v.reciprocal(sml(20 + par), sml(18 + par)))

            @block.tensor
            def _(te):
                te.wait_ge(sP1.h, 1)
                te.wait_ge(sONE.h, 1)
                te.wait_ge(sKMa.h, 1)
                for j in range(4):
                    if j > 0:
                        te.wait_ge(sKMc.h, j)
                    sKMb.inc(te.matmul(QPS[0:1, 0:512], ONESC[:],
                                       KSQ[:, j * 512:(j + 1) * 512],
                                       start=True, stop=True))
                te.wait_ge(sKMe.h, 2)
                sKMb.inc(te.matmul(QPS[:, 0:1], ONES1R[:], KMPX[0:1, 1:2],
                                   start=True, stop=True))

                def emit_qproj(h):
                    te.wait_ge(cQ2[h % 2].h, 16 * (h // 2 + 1))
                    te.wait_ge(sNBF.h, h)
                    if h == 0:
                        te.wait_ge(sKMf.h, 1)
                    for half in range(2):
                        if 2 * h + half >= 1:
                            te.wait_ge(sQF.h, 2 * h + half)
                        for kc in range(KC):
                            mm = te.matmul(
                                QPS[:], WQB[:, h % 2, kc],
                                A[:, kc * 1024 + half * 512:
                                  kc * 1024 + (half + 1) * 512],
                                start=(kc == 0), stop=(kc == KC - 1))
                        sQP.inc(mm)

                def emit_bound(h):
                    te.wait_ge(sQSQ.h, h + 1)
                    te.matmul(QPS[0:33, 0:512], ON_A[:], QSQF[:, 0:512],
                              start=True, stop=False)
                    sQN.inc(te.matmul(QPS[0:33, 0:512], ON_B[:],
                                      QSQF[:, 512:1024],
                                      start=False, stop=True))
                    te.wait_ge(sQRW.h, h + 1)
                    for s8 in range(8):
                        b32 = (s8 // 4) * 32
                        tr = te.matmul(QPS[:, 8 + s8:9 + s8],
                                       QROW[b32:b32 + 1,
                                            (s8 % 4) * 128:(s8 % 4 + 1) * 128],
                                       ONES33[b32:b32 + 1, 0:1],
                                       start=True, stop=True)
                    sQNT.inc(tr)

                def emit_tail(j):
                    h_, sl_ = divmod(j, NS)
                    base_g = h_ * NPTG_H + sum(NPTG_S[:sl_])
                    for gi in range(NPTG_S[sl_]):
                        if base_g + gi >= 2:
                            te.wait_ge(sPRT.h, base_g + gi - 1)
                        if gi == 0:
                            te.wait_ge(sAFM.h, j)
                        n128 = min(8, 2 * sl_ + 2 - gi * 8)
                        for k8 in range(n128):
                            kb = gi * 8 + k8
                            tr = te.transpose(
                                PTP2[gi % 2][:, k8 * 128:(k8 + 1) * 128],
                                PR[:, kb * 128:(kb + 1) * 128], IDB[:])
                        sPT.inc(tr)
                    if j > 0:
                        te.wait_ge(sAVE.h, j)
                    for gi in range(NPTG_S[sl_]):
                        te.wait_ge(sPRT.h, base_g + gi + 1)
                        n128 = min(8, 2 * sl_ + 2 - gi * 8)
                        for k8 in range(n128):
                            kb = gi * 8 + k8
                            mm = te.matmul(AVP[:],
                                           PRT[:, gi * 1024 + k8 * 128:
                                               gi * 1024 + (k8 + 1) * 128],
                                           VT[:, kb, :],
                                           start=(kb == 0),
                                           stop=(kb == 2 * sl_ + 1))
                    sAV.inc(mm)
                    te.wait_ge(sAVE.h, j + 1)
                    sATP.inc(te.transpose(ATP, ATTB[:, j % 2], IDB[:]))

                emit_qproj(0)
                for it in range(HN * NS):
                    h, sl = divmod(it, NS)
                    nb = nblk(sl)
                    if it > 0:
                        te.wait_ge(sEXP.h, it)
                        emit_tail(it - 1)
                    te.wait_ge(sQF.h, 2 * h + 2)
                    te.wait_ge(sQB.h, h + 1)
                    for j in range(nb):
                        te.matmul(SC[:, j * 512:j * 512 + bw(sl, j)],
                                  QFB[:, h % 2, sl * 128:(sl + 1) * 128],
                                  KF[:, j * 512:j * 512 + bw(sl, j)],
                                  start=True, stop=False)
                        mm = te.matmul(SC[:, j * 512:j * 512 + bw(sl, j)],
                                       QB2[:, h % 2, sl * 128:(sl + 1) * 128],
                                       KL2F[:, j * 512:j * 512 + bw(sl, j)],
                                       start=False, stop=True)
                        sSCB.inc(mm)
                    if sl == 0:
                        emit_bound(h)
                    if sl == NS - 1 and h + 1 < HN:
                        emit_qproj(h + 1)
                te.wait_ge(sEXP.h, HN * NS)
                emit_tail(HN * NS - 1)

            if t_q0 is not None:
                @block.sync
                def _(q):
                    q.wait_ge(sQF.h, 2)
                    cT.inc(q.dma_start(out=t_q0[:], in_=QFB[:, 0]), 16)
            if t_wq0 is not None:
                @block.sync
                def _(q):
                    q.wait_ge(cW.h, CW_CONST + 16)
                    cT.inc(q.dma_start(
                        out=t_wq0[:],
                        in_=WQB[:, 0].rearrange("p k j -> p (k j)")), 16)
            if t_afm is not None:
                @block.sync
                def _(q):
                    q.wait_ge(sAFM.h, HN * NS)
                    cT.inc(q.dma_start(out=t_afm[:], in_=B[:]), 16)

        if upto >= 2:
            scope_next("P3")
            p2ps.close()
            p3 = ExitStack()
            ATSTG = p3.enter_context(
                nc.sbuf_tensor("ATSTG", [128, 2, 512], F32, side="right"))
            p3ps = ExitStack()
            WOP = p3ps.enter_context(nc.psum_tensor("WOP", [128, 4096], F32))

            wo_blk0 = len(wsb_sched)
            for jb in range(8):
                for qt in range(4):
                    wsb_push(WO4[jb, qt * 8:(qt + 1) * 8]
                             .rearrange("k p j -> p k j"), 8)

            wo_done = {}

            @block.tensor
            def _(te):
                te.wait_ge(sAFM.h, HN * NS)
                for jb in range(8):
                    for qt in range(4):
                        i = wo_blk0 + jb * 4 + qt
                        te.wait_ge(cB2[i % 2].h, 16 * (i // 2 + 1))
                        if qt == 0 and jb > 0:
                            te.wait_ge(sWOE.h, 8 * jb)
                        for k8 in range(8):
                            kc = qt * 8 + k8
                            for t in range(8):
                                mm = te.matmul(
                                    WOP[:, t * 512:(t + 1) * 512],
                                    B[:, kc * 1024 + t * 128:
                                      kc * 1024 + (t + 1) * 128],
                                    wsb_view(i, 8)[:, k8],
                                    start=(kc == 0), stop=(kc == KC - 1))
                        sWR2.inc(mm)
                        wsb_release[i] = (sWR2.h, sWR2.n)
                    wo_done[jb] = sWR2.n

            @block.scalar
            def _(s):
                for jb in range(8):
                    s.wait_ge(sWR2.h, wo_done[jb])
                    for t in range(8):
                        idx = 8 * jb + t
                        if idx >= 2:
                            s.wait_ge(cS4[idx % 2].h, 16 * (idx // 2))
                        sWOE.inc(s.activation(out=ATSTG[:, idx % 2],
                                              in_=WOP[:, t * 512:(t + 1) * 512],
                                              func=AF.Copy))

            @block.sync
            def _(q):
                for jb in range(8):
                    for t in range(8):
                        idx = 8 * jb + t
                        q.wait_ge(sWOE.h, idx + 1)
                        cS4[idx % 2].inc(q.dma_start(
                            out=SCR_ATT[t][:, jb * 512:(jb + 1) * 512],
                            in_=ATSTG[:, idx % 2]), 16)

        if upto >= 3:
            scope_next("P4")
            p3ps.close()
            p3.close()
            p2.close()
            p1.close()
            p4 = ExitStack()
            LNT2 = p4.enter_context(
                nc.sbuf_tensor("LNT2", [128, 2, D], BF16, side="right"))
            SQJ4 = p4.enter_context(
                nc.sbuf_tensor("SQJ4", [128, D], F32, side="right"))
            p4ps = ExitStack()
            TP2 = [p4ps.enter_context(nc.psum_tensor(f"TP2{k}", [128, 1024],
                                                     BF16)) for k in range(2)]
            ATO = [AV32[:, k * 4096:(k + 1) * 4096] for k in range(2)]
            XSL = [AV32[:, (2 + k) * 4096:(3 + k) * 4096] for k in range(2)]

            @block.sync
            def _(q):
                q.wait_ge(sQP.h, 2 * HN)
                for ss in range(2):
                    q.wait_ge(cS4[ss].h, 16 * 32)
                for sl in range(NS):
                    if sl >= 2:
                        q.wait_ge(sY.h, sl - 1)
                    cA2[sl % 2].inc(
                        q.dma_start(out=ATO[sl % 2], in_=SCR_ATT[sl]), 16)
                    if sl >= 2:
                        q.wait_ge(sLN2.h, sl - 1)
                        q.wait_ge(cY2[sl % 2].h, 16 * (sl // 2))
                    cX2[sl % 2].inc(
                        q.dma_start(out=XSL[sl % 2], in_=XB[2 * sl]), 16)
                    if sl >= 1:
                        q.wait_ge(sY.h, sl)
                        cY2[(sl - 1) % 2].inc(
                            q.dma_start(out=SCR_Y[sl - 1],
                                        in_=XSL[(sl - 1) % 2]), 16)
                q.wait_ge(sY.h, NS)
                cY2[(NS - 1) % 2].inc(
                    q.dma_start(out=SCR_Y[NS - 1], in_=XSL[(NS - 1) % 2]), 16)

            @block.scalar
            def _(s):
                s.wait_ge(sQP.h, 2 * HN)
                for ss in range(2):
                    s.wait_ge(cS4[ss].h, 16 * 32)
                for sl in range(NS):
                    par = sl % 2
                    kato, ky = 16 + 2 * sl, 17 + 2 * sl
                    s.wait_ge(cA2[par].h, 16 * (sl // 2 + 1))
                    ln_act(s, kato, ATO[par], SQJ4[:])
                    s.wait_ge(sY.h, sl + 1)
                    ln_act(s, ky, XSL[par], SQJ4[:])

            @block.vector
            def _(v):
                for sl in range(NS):
                    par = sl % 2
                    kato, ky = 16 + 2 * sl, 17 + 2 * sl
                    v.wait_ge(cA2[par].h, 16 * (sl // 2 + 1))
                    ln_dve_sum(v, kato, ATO[par])
                    ln_dve_var(v, kato)
                    v.wait_ge(sRSTD.h, kato + 1)
                    sT1.inc(v.tensor_scalar(out=ATO[par], in0=ATO[par],
                                            scalar1=sml(12 + kato % 2),
                                            scalar2=sml(14 + kato % 2),
                                            op0=OP.mult, op1=OP.add))
                    v.wait_ge(cX2[par].h, 16 * (sl // 2 + 1))
                    sY.inc(v.tensor_tensor(out=XSL[par], in0=XSL[par],
                                           in1=ATO[par], op=OP.add))
                    ln_dve_sum(v, ky, XSL[par])
                    ln_dve_var(v, ky)
                    v.wait_ge(sRSTD.h, ky + 1)
                    sLN2.inc(v.tensor_scalar(out=LNT2[:, par], in0=XSL[par],
                                             scalar1=sml(12 + ky % 2),
                                             scalar2=sml(14 + ky % 2),
                                             op0=OP.mult, op1=OP.add))
                    for gi in range(4):
                        if SKIP_LN2T:
                            sLNF2.inc(v.memset(sml(38), 0.0))
                            continue
                        v.wait_ge(sTP2.h, 4 * sl + gi + 1)
                        dstap = B[:].rearrange("p (k c) -> p k c", k=KC)[
                            :, gi * 8:(gi + 1) * 8, sl * 128:sl * 128 + 128]
                        srcap = TP2[gi % 2][:].rearrange("p (k c) -> p k c", k=8)
                        sLNF2.inc(v.tensor_copy(dstap, srcap))
                sP4.inc(v.memset(sml(38), 0.0))



            @block.tensor
            def _(te):
                for sl in range(NS):
                    par = sl % 2
                    te.wait_ge(sLN2.h, sl + 1)
                    for gi in range(4):
                        gidx = 4 * sl + gi
                        if SKIP_LN2T:
                            sTP2.inc(te.nop())
                            continue
                        if gidx >= 2:
                            te.wait_ge(sLNF2.h, gidx - 1)
                        for k8 in range(8):
                            kc = gi * 8 + k8
                            tr = te.transpose(
                                TP2[gi % 2][:, k8 * 128:(k8 + 1) * 128],
                                LNT2[:, par, kc * 128:(kc + 1) * 128], IDB[:])
                        sTP2.inc(tr)

            if t_ln2fm is not None:
                @block.sync
                def _(q):
                    q.wait_ge(sP4.h, 1)
                    cT.inc(q.dma_start(out=t_ln2fm[:], in_=B[:]), 16)

        if upto >= 4:
            scope_next("P45")
            p4ps.close()
            p4.close()
            p45 = ExitStack()
            def SBR45(n, s, d):
                return p45.enter_context(nc.sbuf_tensor(n, s, d, side="right"))
            C = SBR45("C", [128, LC * 1024], BF16)
            GL = SBR45("GL", [128, NE], F32)
            EQ = SBR45("EQ", [128, NE], F32)
            G2 = SBR45("G2", [128, NE], F32)
            ROUTE = SBR45("ROUTE", [128, NS], F32)
            EOTM = A[:, 0:NS * 2048]
            EXA = A[:, 16384:16384 + LC * 512].rearrange("p (k c) -> p k c",
                                                         k=LC)

            p45ps = ExitStack()
            GPS = p45ps.enter_context(nc.psum_tensor("GPS", [128, NE], F32))
            F1PS = [p45ps.enter_context(
                nc.psum_tensor(f"F1PS{k}", [128, 1024], F32)) for k in range(2)]
            EXPS = [p45ps.enter_context(
                nc.psum_tensor(f"EXPS{k}", [128, 512], F32)) for k in range(2)]
            TPE = p45ps.enter_context(nc.psum_tensor("TPE", [128, 1024], BF16))

            f1_blk0 = len(wsb_sched)
            for lb in range(LC):
                wsb_push(F14[lb].rearrange("k p j -> p k j"), KC)
            exw_blk0 = len(wsb_sched)
            if P45_PARTS == "full":
                for hf in range(2):
                    wsb_push(EX4[1, hf * 8:(hf + 1) * 8]
                             .rearrange("k p j -> p k j"), 8)

            if P45_PARTS == "full":
                qg_raw_push(EXA[:, 0:8],
                            EX4[0, 0:8].rearrange("k p j -> p k j"), cEXA,
                            [(cY2[0].h, 16 * (NS // 2)),
                             (cY2[1].h, 16 * (NS // 2)), (sLN2.h, NS)])
                qg_raw_push(EXA[:, 8:16],
                            EX4[0, 8:16].rearrange("k p j -> p k j"), cEXA, [])
                qg_raw_push(EXA[:, 0:8],
                            EX4[2, 0:8].rearrange("k p j -> p k j"), cEXA,
                            [(sEX.h, 16)])
                qg_raw_push(EXA[:, 8:16],
                            EX4[2, 8:16].rearrange("k p j -> p k j"), cEXA, [])
                for hf in range(2):
                    wsb_push(EX4[3, hf * 8:(hf + 1) * 8]
                             .rearrange("k p j -> p k j"), 8)

            @block.tensor
            def _(te):
                te.wait_ge(sP4.h, 1)
                for sl in range(NS):
                    if P45_PARTS == "fc1":
                        break
                    if sl > 0:
                        te.wait_ge(sRT.h, sl)
                    for kc in range(KC):
                        mm = te.matmul(GPS[:],
                                       B[:, kc * 1024 + sl * 128:
                                         kc * 1024 + (sl + 1) * 128],
                                       GTB[:, kc * NE:(kc + 1) * NE],
                                       start=(kc == 0), stop=(kc == KC - 1))
                    sGL.inc(mm)
                for lb in range(LC):
                    i = f1_blk0 + lb
                    te.wait_ge(cB2[i % 2].h, 16 * (i // 2 + 1))
                    if lb >= 2:
                        te.wait_ge(sF1E.h, 2 * (lb - 1))
                    for half in range(2):
                        for kc in range(KC):
                            mm = te.matmul(
                                F1PS[lb % 2][:, half * 512:(half + 1) * 512],
                                wsb_view(i, KC)[:, kc],
                                B[:, kc * 1024 + half * 512:
                                  kc * 1024 + (half + 1) * 512],
                                start=(kc == 0), stop=(kc == KC - 1))
                        sF1.inc(mm)
                    wsb_release[i] = (sF1.h, sF1.n)
                te.wait_ge(sF1E.h, 2 * LC)
                for mb in range(4 if P45_PARTS == "full" else 0):
                    if mb % 2 == 0:
                        te.wait_ge(cEXA.h, 32 * (mb // 2 + 1))
                    else:
                        iA = exw_blk0 + (mb // 2) * 2
                        te.wait_ge(cB2[iA % 2].h, 16 * (iA // 2 + 1))
                        te.wait_ge(cB2[(iA + 1) % 2].h, 16 * ((iA + 1) // 2 + 1))
                    for t in range(NS):
                        ei = mb * NS + t
                        if ei >= 2:
                            te.wait_ge(sEXE.h, ei - 1)
                        for lc in range(LC):
                            if mb % 2 == 0:
                                rhs = EXA[:, lc, :]
                            else:
                                iblk = exw_blk0 + (mb // 2) * 2 + lc // 8
                                rhs = wsb_view(iblk, 8)[:, lc % 8]
                            mm = te.matmul(EXPS[ei % 2][:],
                                           C[:, lc * 1024 + t * 128:
                                             lc * 1024 + (t + 1) * 128],
                                           rhs,
                                           start=(lc == 0), stop=(lc == LC - 1))
                        sEX.inc(mm)
                    if mb % 2 == 1 and t == NS - 1:
                        iA = exw_blk0 + (mb // 2) * 2
                        wsb_release[iA] = (sEX.h, sEX.n)
                        wsb_release[iA + 1] = (sEX.h, sEX.n)
                if P45_PARTS == "full":
                    te.wait_ge(sEXE.h, 32)
                    gidx = 0
                    for t in range(NS):
                        for gi in range(2):
                            if gidx >= 1:
                                te.wait_ge(sEOS.h, gidx)
                            for k8 in range(8):
                                mc = gi * 8 + k8
                                tr = te.transpose(
                                    TPE[:, k8 * 128:(k8 + 1) * 128],
                                    EOTM[:, t * 2048 + mc * 128:
                                         t * 2048 + (mc + 1) * 128], IDB[:])
                            sTPE.inc(tr)
                            gidx += 1

            @block.scalar
            def _(s):
                for sl in range(NS if P45_PARTS != "fc1" else 0):
                    s.wait_ge(sGL.h, sl + 1)
                    sGLE.inc(s.activation(out=GL[:], in_=GPS[:], func=AF.Copy))
                    s.wait_ge(sM1.h, sl + 1)
                    sM1B.inc(s.activation(out=sml(35), in_=sml(36),
                                          func=AF.Identity))
                    s.wait_ge(sEQ.h, sl + 1)
                    sG2.inc(s.activation(out=EQ[:], in_=EQ[:],
                                         func=AF.Identity, scale=NEGBIG))
                    s.wait_ge(sM2.h, sl + 1)
                    sG2B.inc(s.activation(out=G2[:], in_=G2[:],
                                          func=AF.Identity))
                    s.wait_ge(sM2R.h, sl + 1)
                    sRT.inc(s.activation(out=ROUTE[:, sl:sl + 1],
                                         in_=sml(37), func=AF.Identity,
                                         bias=sml(35)))

            @block.vector
            def _(v):
                for sl in range(NS if P45_PARTS != "fc1" else 0):
                    v.wait_ge(sGLE.h, sl + 1)
                    sM1.inc(v.tensor_reduce(out=sml(36), in_=GL[:], axis=AX.X,
                                            op=OP.max))
                    v.wait_ge(sM1B.h, sl + 1)
                    sEQ.inc(v.tensor_scalar(out=EQ[:], in0=GL[:],
                                            scalar1=sml(35), scalar2=None,
                                            op0=OP.is_equal, op1=OP.bypass))
                    v.wait_ge(sG2.h, sl + 1)
                    sM2.inc(v.tensor_tensor(out=G2[:], in0=GL[:], in1=EQ[:],
                                            op=OP.add))
                    v.wait_ge(sG2B.h, sl + 1)
                    sM2R.inc(v.tensor_reduce(out=sml(37), in_=G2[:],
                                             axis=AX.X, op=OP.max))
                v.wait_ge(cY2[0].h, 16 * (NS // 2))
                v.wait_ge(cY2[1].h, 16 * (NS // 2))
                for mb in range(4 if P45_PARTS == "full" else 0):
                    for t in range(NS):
                        ei = mb * NS + t
                        v.wait_ge(sEX.h, ei + 1)
                        sEXE.inc(v.tensor_scalar(
                            out=EOTM[:, t * 2048 + mb * 512:
                                 t * 2048 + (mb + 1) * 512],
                            in0=EXPS[ei % 2][:], scalar1=ROUTE[:, t:t + 1],
                            scalar2=None, op0=OP.mult, op1=OP.bypass))
                gidx = 0
                for t in range(NS if P45_PARTS == "full" else 0):
                    for gi in range(2):
                        v.wait_ge(sTPE.h, gidx + 1)
                        dstap = C[:].rearrange("p (k c) -> p k c", k=LC)[
                            :, gi * 8:(gi + 1) * 8, t * 128:t * 128 + 128]
                        srcap = TPE[:].rearrange("p (k c) -> p k c", k=8)
                        sEOS.inc(v.tensor_copy(dstap, srcap))
                        gidx += 1

            @block.scalar
            def _(s):
                for lb in range(LC):
                    for half in range(2):
                        s.wait_ge(sF1.h, 2 * lb + half + 1)
                        sF1E.inc(s.activation(
                            out=C[:, lb * 1024 + half * 512:
                                 lb * 1024 + (half + 1) * 512],
                            in_=F1PS[lb % 2][:, half * 512:(half + 1) * 512],
                            func=AF.Copy))

            if t_route is not None:
                @block.sync
                def _(q):
                    q.wait_ge(sRT.h, NS)
                    cT.inc(q.dma_start(out=t_route[:], in_=ROUTE[:]), 16)
            if t_xlat is not None:
                @block.sync
                def _(q):
                    q.wait_ge(sF1E.h, 2 * LC)
                    cT.inc(q.dma_start(out=t_xlat[:], in_=C[:]), 16)
            if t_eos is not None:
                @block.sync
                def _(q):
                    q.wait_ge(sEOS.h, 16)
                    cT.inc(q.dma_start(out=t_eos[:], in_=C[:]), 16)

        if upto >= 5:
            scope_next("P56")
            p45ps.close()
            p56 = ExitStack()
            def SBR56(n, s, d):
                return p56.enter_context(nc.sbuf_tensor(n, s, d, side="right"))
            ZSTG = SBR56("ZSTG", [128, 2, 512], F32)
            ZPT = SBR56("ZPT", [128, 2, 512], F32)
            RELT = SBR56("RELT", [128, 2, 512], F32)

            up_blk = {}
            dn_blk = {}
            f2_blk = {}
            for half in range(2):
                for ib in range(32):
                    up_blk[(half, ib)] = len(wsb_sched)
                    wsb_push(UP4[half * 32 + ib].rearrange("k p j -> p k j"), KC)
                for fb in range(8):
                    for qt in range(4):
                        dn_blk[(half, fb, qt)] = len(wsb_sched)
                        wsb_push(DN4[fb, half * 32 + qt * 8:
                                     half * 32 + (qt + 1) * 8]
                                 .rearrange("k p j -> p k j"), 8)
                    if half == 1:
                        for hf in range(2):
                            f2_blk[(fb, hf)] = len(wsb_sched)
                            wsb_push(F24[fb, hf * 8:(hf + 1) * 8]
                                     .rearrange("k p j -> p k j"), 8)

            ct_h1_box = [None]

            p5aps = ExitStack()
            UPSA = [p5aps.enter_context(
                nc.psum_tensor(f"UPSA{k}", [128, 512], F32)) for k in range(4)]

            def emit_up(half, UPS, relu_gate_first):
                @block.tensor
                def _(te):
                    if half == 0:
                        te.wait_ge(sEOS.h, 16)
                        te.wait_ge(sF1E.h, 2 * LC)
                        te.wait_ge(sEXE.h, 32)
                    else:
                        te.wait_ge(sZE.h, 64)
                    for ib in range(32):
                        gi = half * 32 + ib
                        i = up_blk[(half, ib)]
                        te.wait_ge(cB2[i % 2].h, 16 * (i // 2 + 1))
                        if ib >= 2:
                            te.wait_ge(sREL.h, half * 64 + 2 * (ib - 1))
                        for hf in range(2):
                            for kc in range(KC):
                                mm = te.matmul(
                                    UPS[2 * (ib % 2) + hf][:],
                                    wsb_view(i, KC)[:, kc],
                                    B[:, kc * 1024 + hf * 512:
                                      kc * 1024 + (hf + 1) * 512],
                                    start=(kc == 0), stop=(kc == KC - 1))
                            sUP.inc(mm)
                        wsb_release[i] = (sUP.h, sUP.n)

                @block.scalar
                def _(s):
                    if half == 0:
                        s.wait_ge(sEOS.h, 16)
                        s.wait_ge(sEX.h, 32)
                    elif ct_h1_box[0] is not None:
                        s.wait_ge(cT.h, ct_h1_box[0])
                    for ib in range(32):
                        for hf in range(2):
                            k = half * 64 + 2 * ib + hf
                            s.wait_ge(sUP.h, k + 1)
                            s.activation(out=RELT[:, k % 2],
                                         in_=UPS[2 * (ib % 2) + hf][:],
                                         func=AF.Relu)
                            sREL.inc(s.activation(
                                out=A[:, ib * 1024 + hf * 512:
                                     ib * 1024 + (hf + 1) * 512],
                                in_=RELT[:, k % 2], func=AF.Square))

            wsb_sched_base = wo_blk0
            scope_next("P5A")
            emit_up(0, UPSA, True)

            p5aps.close()
            p6aps = ExitStack()
            ZPSA = p6aps.enter_context(nc.psum_tensor("ZPSA", [128, 4096], F32))

            dn_done_a = {}

            def p6a():
                @block.tensor
                def _(te):
                    te.wait_ge(sREL.h, 64)
                    for fb in range(8):
                        if fb > 0:
                            te.wait_ge(sZE.h, 8 * fb)
                        for qt in range(4):
                            i = dn_blk[(0, fb, qt)]
                            te.wait_ge(cB2[i % 2].h, 16 * (i // 2 + 1))
                            for k8 in range(8):
                                ic = qt * 8 + k8
                                for t in range(8):
                                    mm = te.matmul(
                                        ZPSA[:, t * 512:(t + 1) * 512],
                                        A[:, ic * 1024 + t * 128:
                                          ic * 1024 + (t + 1) * 128],
                                        wsb_view(i, 8)[:, k8],
                                        start=(ic == 0), stop=(ic == 31))
                            sWR2.inc(mm)
                            wsb_release[i] = (sWR2.h, sWR2.n)
                        dn_done_a[fb] = sWR2.n

                @block.scalar
                def _(s):
                    for fb in range(8):
                        s.wait_ge(sWR2.h, dn_done_a[fb])
                        for t in range(8):
                            idx = 8 * fb + t
                            if idx >= 2:
                                s.wait_ge(cZP2[idx % 2].h, 16 * (idx // 2))
                            sZE.inc(s.activation(
                                out=ZSTG[:, idx % 2],
                                in_=ZPSA[:, t * 512:(t + 1) * 512],
                                func=AF.Copy))

                @block.sync
                def _(q):
                    for fb in range(8):
                        for t in range(8):
                            idx = 8 * fb + t
                            q.wait_ge(sZE.h, idx + 1)
                            cZP2[idx % 2].inc(q.dma_start(
                                out=SCR_ZP[t][:, fb * 512:(fb + 1) * 512],
                                in_=ZSTG[:, idx % 2]), 16)
            scope_next("P6A")
            p6a()

            if t_h1 is not None:
                ct_h1 = cT.n + 16
                @block.sync
                def _(q):
                    q.wait_ge(sREL.h, 64)
                    cT.inc(q.dma_start(out=t_h1[:], in_=A[:]), 16)
                ct_h1_box[0] = ct_h1

            p6aps.close()
            p5bps = ExitStack()
            UPSB = [p5bps.enter_context(
                nc.psum_tensor(f"UPSB{k}", [128, 512], F32)) for k in range(4)]
            scope_next("P5B")
            emit_up(1, UPSB, False)

            scope_next("P6B")
            p5bps.close()
            p6bps = ExitStack()
            ZPSB = p6bps.enter_context(nc.psum_tensor("ZPSB", [128, 4096], F32))

            dn_done_b = {}

            @block.tensor
            def _(te):
                te.wait_ge(sREL.h, 128)
                for fb in range(8):
                    if fb > 0:
                        te.wait_ge(sZE2.h, 8 * fb)
                    for qt in range(4):
                        i = dn_blk[(1, fb, qt)]
                        te.wait_ge(cB2[i % 2].h, 16 * (i // 2 + 1))
                        for k8 in range(8):
                            ic = qt * 8 + k8
                            for t in range(8):
                                mm = te.matmul(
                                    ZPSB[:, t * 512:(t + 1) * 512],
                                    A[:, ic * 1024 + t * 128:
                                      ic * 1024 + (t + 1) * 128],
                                    wsb_view(i, 8)[:, k8],
                                    start=(ic == 0), stop=False)
                        sWR2.inc(mm)
                        wsb_release[i] = (sWR2.h, sWR2.n)
                    for hf in range(2):
                        i = f2_blk[(fb, hf)]
                        te.wait_ge(cB2[i % 2].h, 16 * (i // 2 + 1))
                        for k8 in range(8):
                            lc = hf * 8 + k8
                            for t in range(8):
                                mm = te.matmul(
                                    ZPSB[:, t * 512:(t + 1) * 512],
                                    C[:, lc * 1024 + t * 128:
                                      lc * 1024 + (t + 1) * 128],
                                    wsb_view(i, 8)[:, k8],
                                    start=False, stop=(lc == LC - 1))
                        sWR2.inc(mm)
                        wsb_release[i] = (sWR2.h, sWR2.n)
                    dn_done_b[fb] = sWR2.n

            @block.scalar
            def _(s):
                s.wait_ge(cZP2[0].h, 16 * 32)
                s.wait_ge(cZP2[1].h, 16 * 32)
                for fb in range(8):
                    for t in range(8):
                        idx = 8 * fb + t
                        if idx >= 2:
                            s.wait_ge(sZE2.h, idx - 1)
                        cZL2[idx % 2].inc(s.dma_start(
                            out=ZPT[:, idx % 2],
                            in_=SCR_ZP[t][:, fb * 512:(fb + 1) * 512]), 16)

            @block.sync
            def _(q):
                for fb in range(8):
                    for t in range(8):
                        idx = 8 * fb + t
                        q.wait_ge(sZE2.h, idx + 1)
                        cZ2[idx % 2].inc(q.dma_start(
                            out=SCR_Z[t][:, fb * 512:(fb + 1) * 512],
                            in_=ZSTG[:, idx % 2]), 16)

            @block.vector
            def _(v):
                v.wait_ge(cZP2[0].h, 16 * 32)
                v.wait_ge(cZP2[1].h, 16 * 32)
                for fb in range(8):
                    v.wait_ge(sWR2.h, dn_done_b[fb])
                    for t in range(8):
                        idx = 8 * fb + t
                        v.wait_ge(cZL2[idx % 2].h, 16 * (idx // 2 + 1))
                        if idx >= 2:
                            v.wait_ge(cZ2[idx % 2].h, 16 * (idx // 2))
                        sZE2.inc(v.tensor_tensor(
                            out=ZSTG[:, idx % 2],
                            in0=ZPSB[:, t * 512:(t + 1) * 512],
                            in1=ZPT[:, idx % 2], op=OP.add))


        if upto >= 6:
            scope_next("P7")
            p6bps.close()
            p56.close()
            p45.close()
            p7 = ExitStack()
            GROW = p7.enter_context(
                nc.sbuf_tensor("GROW", [1, D], F32, side="right"))
            ONESR = p7.enter_context(
                nc.sbuf_tensor("ONESR", [1, 128], F32, side="right"))
            SQJ7 = p7.enter_context(
                nc.sbuf_tensor("SQJ7", [128, D], F32, side="right"))
            p7ps = ExitStack()
            BCP = p7ps.enter_context(nc.psum_tensor("BCP", [128, 512], F32))

            ZT = [AV32[:, k * 4096:(k + 1) * 4096] for k in range(2)]
            YT = [AV32[:, (2 + k) * 4096:(3 + k) * 4096] for k in range(2)]
            GAMB = BV32[:, 0:4096]
            BETB = BV32[:, 4096:8192]
            WJ = [BV32[:, (2 + k) * 4096:(3 + k) * 4096] for k in range(2)]

            @block.sync
            def _(q):
                q.wait_ge(sWR2.h, dn_done_b[7])
                q.wait_ge(sUP.h, 128)
                cG.inc(q.dma_start(out=GROW[:], in_=GAMMA[:]), 16)
                q.wait_ge(sGBm.h, 8)
                cG.inc(q.dma_start(out=GROW[:], in_=BETA[:]), 16)

            @block.vector
            def _(v):
                v.wait_ge(sUP.h, 128)
                v.memset(ONESR[:], 1.0)
                sGB.inc(v.memset(sml(38), 0.0))

            @block.scalar
            def _(s):
                for k in range(16):
                    s.wait_ge(sGBm.h, k + 1)
                    dst = GAMB if k < 8 else BETB
                    j8 = k % 8
                    sGBe.inc(s.activation(out=dst[:, j8 * 512:(j8 + 1) * 512],
                                          in_=BCP[:], func=AF.Copy))

            @block.tensor
            def _(te):
                te.wait_ge(sGB.h, 1)
                te.wait_ge(sZE2.h, 64)
                for k in range(16):
                    te.wait_ge(cG.h, 16 if k < 8 else 32)
                    if k > 0:
                        te.wait_ge(sGBe.h, k)
                    j8 = k % 8
                    sGBm.inc(te.matmul(BCP[:], ONESR[:],
                                       GROW[0:1, j8 * 512:(j8 + 1) * 512],
                                       start=True, stop=True))

            @block.sync
            def _(q):
                q.wait_ge(cZ2[0].h, 16 * 32)
                q.wait_ge(cZ2[1].h, 16 * 32)
                for sl in range(NS):
                    if sl >= 2:
                        q.wait_ge(sTG.h, sl - 1)
                    cA2[sl % 2].inc(q.dma_start(out=ZT[sl % 2],
                                                in_=SCR_Z[sl]), 16)
                    if sl >= 1:
                        q.wait_ge(sO.h, sl)
                        cO2[(sl - 1) % 2].inc(
                            q.dma_start(out=OUT[sl - 1],
                                        in_=YT[(sl - 1) % 2]), 16)
                q.wait_ge(sO.h, NS)
                cO2[(NS - 1) % 2].inc(
                    q.dma_start(out=OUT[NS - 1], in_=YT[(NS - 1) % 2]), 16)
                q.wait_ge(cO2[0].h, 16 * (NS // 2))
                q.wait_ge(cO2[1].h, 16 * (NS // 2))

            @block.scalar
            def _(s):
                for sl in range(NS):
                    par = sl % 2
                    kz, kw = 32 + 2 * sl, 33 + 2 * sl
                    if sl >= 2:
                        s.wait_ge(cO2[par].h, 16 * (sl // 2))
                    cX2[par].inc(s.dma_start(out=YT[par], in_=SCR_Y[sl]), 16)
                    s.wait_ge(cA2[par].h, 16 * (NS // 2 + sl // 2 + 1))
                    ln_act(s, kz, ZT[par], SQJ7[:])
                    s.wait_ge(sWW.h, sl + 1)
                    ln_act(s, kw, WJ[par], SQJ7[:])
                    sUN.inc(s.activation(out=WJ[par], in_=WJ[par],
                                         func=AF.Identity,
                                         scale=sml(12 + kw % 2),
                                         bias=sml(14 + kw % 2)))

            @block.vector
            def _(v):
                v.wait_ge(sGBe.h, 16)
                for sl in range(NS):
                    par = sl % 2
                    kz, kw = 32 + 2 * sl, 33 + 2 * sl
                    v.wait_ge(cA2[par].h, 16 * (NS // 2 + sl // 2 + 1))
                    ln_dve_sum(v, kz, ZT[par])
                    ln_dve_var(v, kz)
                    v.wait_ge(sRSTD.h, kz + 1)
                    sIA.inc(v.tensor_scalar(out=ZT[par], in0=ZT[par],
                                            scalar1=sml(12 + kz % 2),
                                            scalar2=sml(14 + kz % 2),
                                            op0=OP.mult, op1=OP.add))
                    if sl >= 2:
                        v.wait_ge(sUN.h, sl - 1)
                    sTG.inc(v.tensor_tensor(out=WJ[par], in0=ZT[par],
                                            in1=GAMB, op=OP.mult))
                    sWW.inc(v.tensor_tensor(out=WJ[par], in0=WJ[par],
                                            in1=BETB, op=OP.add))
                    ln_dve_sum(v, kw, WJ[par])
                    ln_dve_var(v, kw)
                    v.wait_ge(sUN.h, sl + 1)
                    v.wait_ge(cX2[par].h, 16 * (NS // 2 + sl // 2 + 1))
                    sO.inc(v.tensor_tensor(out=YT[par], in0=YT[par],
                                           in1=WJ[par], op=OP.add))



        if "cur" in _scope_state:
            nc.leave_named_scope(_scope_state["cur"][0], _scope_state["cur"][1],
                                 True)
            _scope_state.pop("cur")

        if qg_sched:
            @block.gpsimd
            def _(q):
                for kind, ent in qg_sched:
                    if kind == "wsb":
                        i = ent
                        srcap, kdim = wsb_sched[i]
                        if i >= 2:
                            q.wait_ge(*wsb_release[i - 2])
                        cB2[i % 2].inc(
                            q.dma_start(out=wsb_view(i, kdim), in_=srcap), 16)
                    else:
                        dst_ap, src_ap, sem, waits = ent
                        for wh, wn in waits:
                            q.wait_ge(wh, wn)
                        sem.inc(q.dma_start(out=dst_ap, in_=src_ap), 16)

        @block.sync
        def _(q):
            for sm in ([cT, cG] + cXL2 + cY2
                       + cZ2 + cO2 + cA2 + cS4):
                q.wait_ge(sm.h, sm.n)

        @block.gpsimd
        def _(q):
            for sm in [cW, cEXA] + cQ2 + cB2:
                q.wait_ge(sm.h, sm.n)

        @block.scalar
        def _(s):
            for sm in cX2 + cZL2 + cZP2:
                s.wait_ge(sm.h, sm.n)

    return nc, taps_out


def _host_inputs(core, shared, x_full):
    b, r = core // 2, core % 2
    xt = np.asarray(x_full)[b].reshape(NT, 128, D)
    order = []
    for j in range(NS):
        order.append(2 * j + r)
        order.append(2 * j + (1 - r))
    XBc = np.ascontiguousarray(xt[order])
    kpos = np.empty(S2, np.float32)
    for p, gt in enumerate(order):
        kpos[p * 128:(p + 1) * 128] = np.arange(gt * 128, (gt + 1) * 128,
                                                dtype=np.float32)
    KL2c = np.ascontiguousarray(np.stack([kpos,
                                          np.full(S2, -1.0, np.float32)]))
    qpos = np.empty(1024, np.float32)
    for s in range(NS):
        gt = 2 * s + r
        qpos[s * 128:(s + 1) * 128] = np.arange(gt * 128, (gt + 1) * 128,
                                                dtype=np.float32)
    Q2c = np.ascontiguousarray(np.stack([np.ones(1024, np.float32), qpos]))
    tri = np.where(np.arange(128)[None, :] > np.arange(128)[:, None],
                   np.float32(NEGBIG), np.float32(0.0)).astype(np.float32)
    mrol = (np.full((128, 128), NEGBIG, np.float32) if r == 0
            else np.zeros((128, 128), np.float32))
    d = {"XB": XBc, "KL2": KL2c, "Q2": Q2c,
         "MTRI": np.ascontiguousarray(tri), "MROL": np.ascontiguousarray(mrol)}
    d.update(shared)
    return d


def _shared_weights(i):
    def tile4(w, fblk):
        w = np.asarray(w)
        K, F = w.shape
        return np.ascontiguousarray(
            w.reshape(K // 128, 128, F // fblk, fblk).transpose(2, 0, 1, 3))
    return {
        "WQ4": tile4(i["L0_attn_Wq"], 128),
        "WKV4": tile4(i["L0_attn_Wkv"], 128),
        "WO4": tile4(i["L0_attn_Wo"], 512),
        "UP4": tile4(i["L0_ffn_up_w"], 128),
        "DN4": tile4(i["L0_ffn_down_w"], 512),
        "F14": tile4(i["L0_ffn_fc1_w"], 128),
        "EX4": tile4(i["L0_ffn_expert_w"], 512),
        "F24": tile4(i["L0_ffn_fc2_w"], 512),
        "GTH": np.ascontiguousarray(
            np.asarray(i["L0_ffn_gate_w"]).reshape(KC, 128, NE)
            .transpose(1, 0, 2).reshape(128, KC * NE)),
        "GAMMA": np.ascontiguousarray(
            np.asarray(i["L0_ffn_ln_gamma"], np.float32)[None, :]),
        "BETA": np.ascontiguousarray(
            np.asarray(i["L0_ffn_ln_beta"], np.float32)[None, :]),
        "IDBH": np.eye(128).astype(ml_dtypes.bfloat16),
    }


def kernel(**inputs):
    shared = _shared_weights(inputs)
    x_full = np.asarray(inputs["L0_attn_hidden_states"], np.float32)
    in_maps = [_host_inputs(c, shared, x_full) for c in range(8)]
    nc, taps_out = build_nc()
    res = run_bass_kernel_spmd(nc, in_maps, core_ids=list(range(8)))
    out = np.empty_like(x_full)
    for c in range(8):
        b, r = c // 2, c % 2
        o = np.asarray(res.results[c]["OUT"])
        for s in range(NS):
            gt = 2 * s + r
            out[b, gt * 128:(gt + 1) * 128, :] = o[s]
    kernel.last_results = res.results if TAPS else None
    return out
